# version 03
# baseline (speedup 1.0000x reference)
import math
from contextlib import ExitStack

import numpy as np
import concourse.bass as bass
import concourse.mybir as mybir
from concourse.bass_utils import run_bass_kernel_spmd

F32 = mybir.dt.float32
BF16 = mybir.dt.bfloat16
F16 = mybir.dt.float16
U8 = mybir.dt.uint8
AF = mybir.ActivationFunctionType
ALU = mybir.AluOpType
AXX = mybir.AxisListType.X

T = 1024
D = 4096
NCH = 32
NH = 32
QSCALE = 1.0 / math.sqrt(128.0)
MARGIN2 = 1.05
EPS = 1e-5
INV_D = 1.0 / 4096.0
PI_OVER = math.pi / 2048.0


def _m16_table():
    n = 4096
    t = np.arange(n, dtype=np.int64)
    M = np.zeros((n, n), np.int64)
    M[:, 0] = 1024
    f = np.arange(1, (n - 1) // 2 + 1, dtype=np.int64)
    ft = t[:, None] * f[None, :]
    M[:, 2 * f - 1] = ft + 1024
    M[:, 2 * f] = ft
    M[:, n - 1] = 2048 * t + 1024
    M = ((M + 2048) % 4096) - 2048
    MT = np.ascontiguousarray(M.T).astype(np.float16)
    s = np.full(n, math.sqrt(2.0 / n), np.float64)
    s[0] = 1.0 / math.sqrt(n)
    s[n - 1] = 1.0 / math.sqrt(n)
    svec = np.ascontiguousarray((0.5 * s).reshape(32, 128).T).astype(np.float32)
    return MT.reshape(32, 128, 4096), svec


class Sem:
    def __init__(self, stack, nc, name):
        self.h = stack.enter_context(nc.semaphore(name))
        self.n = 0

    def take(self, k):
        self.n += k
        return self.n


def build_nc(stop_phase=None):
    nc = bass.Bass("TRN2", target_bir_lowering=False, detect_race_conditions=False)

    xT = nc.dram_tensor("xT", [D, T], F32, kind="ExternalInput").ap()
    wqkv = nc.dram_tensor("wqkv", [96, 128, NCH * 128], F32, kind="ExternalInput").ap()
    wb = nc.dram_tensor("wb", [32, 128, NCH * 128], F32, kind="ExternalInput").ap()
    wo = nc.dram_tensor("wo", [32, 128, NCH * 128], F32, kind="ExternalInput").ap()
    m16 = nc.dram_tensor("m16", [32, 128, 4096], F16, kind="ExternalInput").ap()
    encr = nc.dram_tensor("encr", [128, NCH * 40], F32, kind="ExternalInput").ap()
    coefr = nc.dram_tensor("coefr", [128, NCH * 3 * 40], F32, kind="ExternalInput").ap()
    maskr = nc.dram_tensor("maskr", [128, NCH * 3 * 40], U8, kind="ExternalInput").ap()
    svec_in = nc.dram_tensor("svec", [128, 32], F32, kind="ExternalInput").ap()
    qkvb_c = nc.dram_tensor("qkvb_c", [128, 96], F32, kind="ExternalInput").ap()
    bb_c = nc.dram_tensor("bb_c", [128, 32], F32, kind="ExternalInput").ap()
    ob_c = nc.dram_tensor("ob_c", [128, 32], F32, kind="ExternalInput").ap()
    yT = nc.dram_tensor("yT", [D, T], F32, kind="ExternalOutput").ap()
    if stop_phase == "fulldbg":
        dbg = nc.dram_tensor("dbg", [D, T], F32, kind="ExternalOutput").ap()
        dbg2 = nc.dram_tensor("dbg2", [D, T], F32, kind="ExternalOutput").ap()
        dbg3 = nc.dram_tensor("dbg3", [132, T], F32, kind="ExternalOutput").ap()

    x1 = nc.dram_tensor("x1", [D, T], F32).ap()
    x2 = nc.dram_tensor("x2", [D, T], F32).ap()
    qspill = nc.dram_tensor("qspill", [32, 128, T], BF16).ap()
    kv_in = nc.dram_tensor("kv_in", [8192, T], BF16)
    kv_out = nc.dram_tensor("kv_out", [4 * 8192, T], BF16)
    km_in = nc.dram_tensor("km_in", [1, 1], F32)
    km_out = nc.dram_tensor("km_out", [8, 1], F32)

    st = ExitStack()
    sbt = lambda name, shape, dt: st.enter_context(nc.sbuf_tensor(name, shape, dt))

    bigA = sbt("bigA", [128, NCH * T], BF16)
    bigB = sbt("bigB", [128, NCH * T], BF16)
    fb = sbt("fb", [128, 7680], F32)
    pbuf = sbt("pbuf", [128, 2 * 4096], BF16)
    kvb = sbt("kvb", [128, 2 * 4096], BF16)
    qh = sbt("qh", [128, 2 * T], BF16)
    gt = sbt("gt", [40, 4096], BF16)
    encs = sbt("encs", [128, NCH * 40], BF16)
    onec = sbt("onec", [128, 1], BF16)
    oner = sbt("oner", [1, 128], F32)
    biasq = sbt("biasq", [128, 96], F32)
    biasqs = sbt("biasqs", [128, 32], F32)
    biasbh = sbt("biasbh", [128, 32], F32)
    biaso = sbt("biaso", [128, 32], F32)
    small = sbt("small", [128, 64], F32)
    sumab = sbt("sumab", [128, 20], F32)

    psA = st.enter_context(nc.psum_tensor("psA", [128, 2048], F32))
    psB = st.enter_context(nc.psum_tensor("psB", [128, 2048], F32))

    pan = [bigB[:, 0:4096], bigB[:, 4096:8192]]
    ksq = bigB[:, 8192:9216]
    qsq = bigB[:, 8192:9216]
    xsq = [bigB[:, 9216 + 0 * T: 9216 + 1 * T],
           bigB[:, 9216 + 1 * T: 9216 + 2 * T],
           bigB[:, 8192:9216]]
    esb = bigB[0:40, 11264:12288]
    stag = [bigB[:, 12288 + i * T: 12288 + (i + 1) * T] for i in range(2)]
    vstg = [bigB[:, 14336 + i * T: 14336 + (i + 1) * T] for i in range(2)]
    msb = [bigB[:, 16384 + i * 4096: 16384 + (i + 1) * 4096].bitcast(F16) for i in range(2)]
    fsb = [bigB[:, 24576 + i * 4096: 24576 + (i + 1) * 4096] for i in range(2)]
    masku = bigB[:, 0:1920].bitcast(U8)
    xres = [fb[:, i * T:(i + 1) * T] for i in range(2)]
    ysb = [fb[:, 2048 + i * T: 2048 + (i + 1) * T] for i in range(2)]
    Ab = fb[:, 4096:5120]
    Bb = fb[:, 5120:6144]
    tsc = fb[:, 6144:7168]
    coeff = fb[:, 0:3840]
    gsb = fb[:, 0:640].bitcast(BF16)
    maskf = fb[:, 3840:7680]
    meanr = fb[0:1, 0:1024]
    varr = fb[0:1, 1024:2048]
    rstdr = fb[0:1, 2048:3072]
    brow = fb[0:1, 3072:4096]
    kmax2c = small[:, 0:1]
    negc = small[:, 8:16]
    csq = small[:, 16:24]
    ccol = small[:, 24:32]
    svecs = small[:, 32:64]
    kmx = small[0:1, 48:49]
    kred = small[0:1, 49:50]
    kmg = small[0:1, 50:51]
    km8 = small[0:1, 52:60]
    epsc = small[0:1, 61:62]
    kbuf = kvb[:, 0:4096]
    vbuf = kvb[:, 4096:8192]

    S = lambda name: Sem(st, nc, name)
    gld, sld, tld = S("gld"), S("sld"), S("tld")
    stx, sty = S("stx"), S("sty")
    pe, act, dve = S("pe"), S("act"), S("dve")
    cc1, cc2 = S("cc1"), S("cc2")

    prog = {k: [] for k in ("sync", "gpsimd", "scalar", "vector", "tensor")}

    def on(eng, fn):
        prog[eng].append(fn)

    def wait(eng, *pairs):
        pairs = tuple((s.h, v) for s, v in pairs if v > 0)
        if pairs:
            on(eng, lambda e, p=pairs: [e.wait_ge(sh, v) for sh, v in p])

    def p0(e):
        e.dma_start(out=encs[:, :], in_=encr[:, :]).then_inc(gld.h, 16)
        e.dma_start(out=coeff, in_=coefr[:, :]).then_inc(gld.h, 16)
        e.dma_start(out=masku, in_=maskr[:, :]).then_inc(gld.h, 16)
        e.dma_start(out=svecs, in_=svec_in[:, :]).then_inc(gld.h, 16)
        e.dma_start(out=biasq[:, :], in_=qkvb_c[:, :]).then_inc(gld.h, 16)
        e.dma_start(out=biasbh[:, :], in_=bb_c[:, :]).then_inc(gld.h, 16)
        e.dma_start(out=biaso[:, :], in_=ob_c[:, :]).then_inc(gld.h, 16)
    on("gpsimd", p0)
    gld.take(16 * 7)
    P0_DONE = gld.n

    def p1(e):
        cf = coeff.rearrange("p (oc ee r) -> p oc ee r", oc=32, ee=3)
        mk = masku.rearrange("p (oc ee r) -> p oc ee r", oc=32, ee=3)
        mf = maskf.rearrange("p (oc ee r) -> p oc ee r", oc=32, ee=3)
        e.tensor_copy(mf[:, :, :, :], mk[:, :, :, :])
        e.tensor_tensor(mf[:, :, :, :], cf[:, :, :, :], mf[:, :, :, :], ALU.mult)
        e.tensor_tensor(mf[:, :, 0, :], mf[:, :, 0, :], mf[:, :, 1, :], ALU.add)
        e.tensor_tensor(mf[:, :, 0, :], mf[:, :, 0, :], mf[:, :, 2, :], ALU.add)
        for oc in range(32):
            e.tensor_scalar(gsb[:, oc * 40:(oc + 1) * 40], mf[:, oc, 0, :],
                            svecs[:, oc:oc + 1], None, ALU.mult)
        e.memset(onec[:, :], 1.0)
        e.memset(oner[:, :], 1.0)
        e.memset(epsc, EPS)
        e.tensor_scalar(biasqs[:, :], biasq[:, 0:32], QSCALE, None, ALU.mult)
        e.tensor_scalar(biasbh[:, :], biasbh[:, :], 0.5, None,
                        ALU.mult).then_inc(dve.h, 1)
    wait("vector", (gld, P0_DONE))
    on("vector", p1)
    GPREP = dve.take(1)

    sin_done, gmm_done = [], []
    for oc in range(32):
        par = oc % 2
        if oc >= 2:
            wait("sync", (act, sin_done[oc - 2]))
        on("sync", lambda e, oc=oc, par=par: e.dma_start(
            out=msb[par][:, :], in_=m16[oc, :, :]).then_inc(sld.h, 16))
        ml = sld.take(16)
        wait("scalar", (sld, ml))
        if oc >= 2:
            wait("scalar", (pe, gmm_done[oc - 2]))
        on("scalar", lambda e, par=par: e.activation(
            fsb[par][:, :], msb[par][:, :], AF.Sin, scale=PI_OVER).then_inc(act.h, 1))
        sin_done.append(act.take(1))
        wait("tensor", (act, sin_done[oc]), (dve, GPREP))
        def gmm(e, oc=oc, par=par):
            for ts in range(8):
                ps = (psA if ts < 4 else psB)[0:40, (ts % 4) * 512:(ts % 4 + 1) * 512]
                last = e.matmul(ps, gsb[:, oc * 40:(oc + 1) * 40],
                                fsb[par][:, ts * 512:(ts + 1) * 512],
                                start=(oc == 0), stop=(oc == 31), skip_group_check=True)
            last.then_inc(pe.h, 1)
        on("tensor", gmm)
        gmm_done.append(pe.take(1))
    wait("vector", (pe, gmm_done[-1]))
    def gt_evac(e):
        for ts in range(8):
            ps = (psA if ts < 4 else psB)[0:40, (ts % 4) * 512:(ts % 4 + 1) * 512]
            e.tensor_copy(gt[0:40, ts * 512:(ts + 1) * 512], ps)
        e.tensor_copy(small[0:40, 56:60], gt[0:40, 4092:4096]).then_inc(dve.h, 1)
    on("vector", gt_evac)
    GT_READY = dve.take(1)

    def ln_and_hat(src_dram, store_gate=None):
        loads = []
        for dc in range(NCH):
            if store_gate is not None and dc == 0:
                wait("gpsimd", (store_gate[0], store_gate[1]))
            on("gpsimd", lambda e, dc=dc: e.dma_start(
                out=bigA[:, dc * T:(dc + 1) * T],
                in_=src_dram[dc * 128:(dc + 1) * 128, :]).then_inc(gld.h, 16))
            loads.append(gld.take(16))
        on("gpsimd", lambda e: e.dma_start(
            out=bigA[:, (NCH - 1) * T:NCH * T],
            in_=src_dram[(NCH - 1) * 128:NCH * 128, :]).then_inc(gld.h, 16))
        loads.append(gld.take(16))
        stats_mm = []
        sq_done = []
        for it in range(NCH + 1):
            if it < NCH:
                dc_s = it
                wait("scalar", (gld, loads[dc_s + 1]))
                if dc_s >= 3:
                    wait("scalar", (pe, stats_mm[dc_s - 3]))
                def sq_em(e, par=dc_s % 3, dc=dc_s):
                    e.activation(xsq[par][:, :], bigA[:, dc * T:(dc + 1) * T], AF.Square)
                    e.activation(small[:, 56:60], xsq[par][:, 1020:1024],
                                 AF.Identity).then_inc(act.h, 1)
                on("scalar", sq_em)
                sq_done.append(act.take(1))
            else:
                wait("scalar", (pe, stats_mm[NCH - 3]))
                on("scalar", lambda e: e.activation(
                    xsq[NCH % 3][:, :], bigA[:, 0:T], AF.Square).then_inc(act.h, 1))
                sq_done.append(act.take(1))
            if it == 0:
                continue
            dc = it - 1
            par = dc % 3
            sqd = sq_done[it]
            if dc == 0:
                wait("tensor", (act, sqd), (dve, GT_READY))
            else:
                wait("tensor", (act, sqd))
            def smm(e, dc=dc, par=par):
                for h2 in range(2):
                    e.matmul(psB[0:1, h2 * 512:(h2 + 1) * 512], onec[:, :],
                             bigA[:, dc * T + h2 * 512: dc * T + (h2 + 1) * 512],
                             start=(dc == 0), stop=(dc == NCH - 1), skip_group_check=True)
                    e.matmul(psB[0:1, 1024 + h2 * 512: 1024 + (h2 + 1) * 512],
                             onec[:, :], xsq[par][:, h2 * 512:(h2 + 1) * 512],
                             start=(dc == 0), stop=(dc == NCH - 1), skip_group_check=True)
                if dc == NCH - 1:
                    e.matmul(psA[0:1, 0:512], onec[:, :], bigA[:, 0:512],
                             start=True, stop=True, skip_group_check=True)
                    e.matmul(psA[0:1, 0:512], onec[:, :], bigA[:, 0:512],
                             start=True, stop=True, skip_group_check=True)
                    last = e.matmul(psA[0:1, 0:512], onec[:, :], bigA[:, 0:512],
                                    start=True, stop=True, skip_group_check=True)
                else:
                    last = e.matmul(psA[0:1, 0:1], onec[:, :], onec[:, :], start=True,
                                    stop=True, skip_group_check=True)
                last.then_inc(pe.h, 1)
            on("tensor", smm)
            stats_mm.append(pe.take(1))
        wait("vector", (pe, stats_mm[-1]))
        vs = [dve.take(1) for _ in range(4)]
        def sm1(e, vs=vs):
            e.tensor_scalar(meanr, psB[0:1, 0:1024], INV_D, None, ALU.mult).then_inc(dve.h, 1)
            e.wait_ge(dve.h, vs[0])
            e.tensor_scalar(varr, psB[0:1, 1024:2048], INV_D, None, ALU.mult).then_inc(dve.h, 1)
            e.wait_ge(dve.h, vs[1])
            e.tensor_tensor(brow, meanr, meanr, ALU.mult).then_inc(dve.h, 1)
            e.wait_ge(dve.h, vs[2])
            e.tensor_tensor(varr, varr, brow, ALU.subtract).then_inc(dve.h, 1)
            e.wait_ge(dve.h, vs[3])
            e.tensor_copy(small[0:1, 56:60], varr[0:1, 1020:1024]).then_inc(dve.h, 1)
        on("vector", sm1)
        v1 = dve.take(1)
        wait("scalar", (dve, v1))
        def sa1(e):
            e.activation(rstdr, varr, AF.Ln, bias=epsc)
            e.activation(rstdr, rstdr, AF.Exp, scale=-0.5)
            e.activation(small[0:1, 56:60], rstdr[0:1, 1020:1024],
                         AF.Identity).then_inc(act.h, 1)
        on("scalar", sa1)
        a1 = act.take(1)
        wait("vector", (act, a1))
        vb = dve.take(1)
        def bm(e, vb=vb):
            e.tensor_tensor(brow, meanr, rstdr, ALU.mult).then_inc(dve.h, 1)
            e.wait_ge(dve.h, vb)
            e.tensor_copy(small[0:1, 56:60], brow[0:1, 1020:1024]).then_inc(dve.h, 1)
        on("vector", bm)
        b1 = dve.take(1)
        wait("tensor", (dve, b1))
        def bc(e):
            for h2 in range(2):
                e.matmul(psB[:, h2 * 512:(h2 + 1) * 512], oner[:, :],
                         rstdr[:, h2 * 512:(h2 + 1) * 512], start=True, stop=True,
                         skip_group_check=True)
                e.matmul(psB[:, 1024 + h2 * 512: 1024 + (h2 + 1) * 512], oner[:, :],
                         brow[:, h2 * 512:(h2 + 1) * 512], start=True, stop=True,
                         skip_group_check=True)
            e.matmul(psA[0:1, 0:1], onec[:, :], onec[:, :], start=True, stop=True,
                     skip_group_check=True)
            last = e.matmul(psA[0:1, 0:1], onec[:, :], onec[:, :], start=True,
                            stop=True, skip_group_check=True)
            last.then_inc(pe.h, 1)
        on("tensor", bc)
        bc_done = pe.take(1)
        wait("vector", (pe, bc_done))
        def bce(e):
            e.tensor_copy(Ab[:, :], psB[:, 0:1024])
            e.tensor_copy(Bb[:, :], psB[:, 1024:2048])
            e.tensor_copy(small[:, 56:58], Ab[:, 1022:1024])
            e.tensor_copy(small[:, 58:60], Bb[:, 1022:1024]).then_inc(dve.h, 1)
        on("vector", bce)
        bce_done = dve.take(1)
        hat = []
        for dc in range(NCH):
            if dc == 0:
                wait("vector", (dve, bce_done))
            def xh(e, dc=dc):
                e.tensor_tensor(tsc[:, :], bigA[:, dc * T:(dc + 1) * T], Ab[:, :], ALU.mult)
                e.tensor_tensor(bigA[:, dc * T:(dc + 1) * T], tsc[:, :], Bb[:, :],
                                ALU.subtract)
                e.tensor_copy(small[:, 56:60], bigA[:, dc * T + 1020: dc * T + 1024]
                              ).then_inc(dve.h, 1)
            on("vector", xh)
            hat.append(dve.take(1))
        return {"hat": hat, "bc_done": bc_done, "stats_mm": stats_mm}

    def ffn(src_dram, dst_dram, dst_sem, store_gate=None, xres_extra=(), stop_at_hat=False, dump_dbg=False):
        ms = ln_and_hat(src_dram, store_gate=store_gate)
        hat_all = ms["hat"][-1]
        if stop_at_hat:
            wait("gpsimd", (dve, hat_all))
            b_sty = sty.n
            def mvh3(e, b_sty=b_sty):
                for dc in range(NCH):
                    e.dma_start(out=yT[dc * 128:(dc + 1) * 128, :],
                                in_=bigA[:, dc * T:(dc + 1) * T]).then_inc(sty.h, 16)
                e.wait_ge(sty.h, b_sty + NCH * 16)
            on("gpsimd", mvh3)
            sty.take(16 * NCH)
            return None
        if dump_dbg:
            wait("sync", (dve, ms["hat"][0]))
            b_sty3 = sty.n
            def mvrows(e, b_sty3=b_sty3):
                e.dma_start(out=dbg3[0:1, :], in_=meanr).then_inc(sty.h, 16)
                e.dma_start(out=dbg3[1:2, :], in_=varr).then_inc(sty.h, 16)
                e.dma_start(out=dbg3[2:3, :], in_=rstdr).then_inc(sty.h, 16)
                e.dma_start(out=dbg3[3:4, :], in_=brow).then_inc(sty.h, 16)
                e.dma_start(out=dbg3[4:132, :], in_=Ab[:, :]).then_inc(sty.h, 16)
                e.wait_ge(sty.h, b_sty3 + 80)
            on("sync", mvrows)
            sty.take(80)
            wait("gpsimd", (dve, hat_all))
            b_gld = gld.n
            def mvdbg(e, b_gld=b_gld):
                for dc in range(NCH):
                    e.dma_start(out=dbg[dc * 128:(dc + 1) * 128, :],
                                in_=bigA[:, dc * T:(dc + 1) * T]).then_inc(gld.h, 16)
            on("gpsimd", mvdbg)
            DBG_SNAP = gld.take(16 * NCH)
            wait("tensor", (gld, DBG_SNAP))
        wait("tensor", (dve, hat_all))
        def emm(e):
            for dc in range(NCH):
                for h2 in range(2):
                    e.matmul(psB[0:40, h2 * 512:(h2 + 1) * 512],
                             encs[:, dc * 40:(dc + 1) * 40],
                             bigA[:, dc * T + h2 * 512: dc * T + (h2 + 1) * 512],
                             start=(dc == 0), stop=(dc == NCH - 1),
                             skip_group_check=True)
            e.matmul(psA[0:1, 0:1], onec[:, :], onec[:, :], start=True, stop=True,
                     skip_group_check=True)
            last = e.matmul(psA[0:1, 0:1], onec[:, :], onec[:, :], start=True,
                            stop=True, skip_group_check=True)
            last.then_inc(pe.h, 1)
        on("tensor", emm)
        e_done = pe.take(1)
        wait("vector", (pe, e_done))
        def esbc(e):
            e.tensor_copy(esb[:, :], psB[0:40, 0:1024])
            e.tensor_copy(small[0:40, 56:60], esb[0:40, 1020:1024]).then_inc(dve.h, 1)
        on("vector", esbc)
        esb_done = dve.take(1)

        mm_done = [0] * NCH
        evac_done = [0] * NCH
        st_done = [0] * NCH
        for oc in range(NCH):
            par = oc % 2
            if oc >= 2:
                wait("gpsimd", (pe, mm_done[oc - 2]))
            def panld(e, oc=oc, par=par):
                e.dma_start(out=pan[par][:, :], in_=wb[oc, :, :]).then_inc(gld.h, 16)
                e.dma_start(out=pan[par][:, 3968:4096],
                            in_=wb[oc, :, 3968:4096]).then_inc(gld.h, 16)
            on("gpsimd", panld)
            wl = gld.take(32)
            if oc >= 2:
                wait("sync", (dve, evac_done[oc - 2]))
            elif xres_extra:
                wait("sync", *xres_extra)
            if oc < 2:
                wait("sync", (pe, ms["bc_done"]))
            def xrl(e, oc=oc, par=par):
                e.dma_start(out=xres[par][:, :],
                            in_=src_dram[oc * 128:(oc + 1) * 128, :]).then_inc(sld.h, 16)
                e.dma_start(out=xres[par][:, 896:1024],
                            in_=src_dram[oc * 128:(oc + 1) * 128, 896:1024]
                            ).then_inc(sld.h, 16)
            on("sync", xrl)
            xl = sld.take(32)
            wait("tensor", (gld, wl), (dve, esb_done))
            if oc >= 2:
                wait("tensor", (dve, evac_done[oc - 2]))
            def fmm(e, oc=oc, par=par):
                po = psA[:, par * 1024:(par + 1) * 1024]
                for dc in range(NCH):
                    for h2 in range(2):
                        e.matmul(po[:, h2 * 512:(h2 + 1) * 512],
                                 pan[par][:, dc * 128:(dc + 1) * 128],
                                 bigA[:, dc * T + h2 * 512: dc * T + (h2 + 1) * 512],
                                 start=(dc == 0), stop=False, skip_group_check=True)
                for h2 in range(2):
                    last = e.matmul(po[:, h2 * 512:(h2 + 1) * 512],
                             gt[0:40, oc * 128:(oc + 1) * 128],
                             esb[0:40, h2 * 512:(h2 + 1) * 512],
                             start=False, stop=True, skip_group_check=True)
                last.then_inc(pe.h, 1)
            on("tensor", fmm)
            mm_done[oc] = pe.take(1)
            if oc >= 2:
                wait("vector", (pe, mm_done[oc]), (sld, xl), (dst_sem, st_done[oc - 2]))
            else:
                wait("vector", (pe, mm_done[oc]), (sld, xl))
            def fev(e, oc=oc, par=par):
                po = psA[:, par * 1024:(par + 1) * 1024]
                e.tensor_scalar(ysb[par][:, :], po, 0.5, biasbh[:, oc:oc + 1],
                                ALU.mult, ALU.add)
                e.tensor_tensor(ysb[par][:, :], ysb[par][:, :], xres[par][:, :],
                                ALU.add)
                e.tensor_copy(small[:, 56:60], ysb[par][:, 1020:1024]).then_inc(dve.h, 1)
            on("vector", fev)
            evac_done[oc] = dve.take(1)
            wait("sync", (dve, evac_done[oc]))
            on("sync", lambda e, oc=oc, par=par: e.dma_start(
                out=dst_dram[oc * 128:(oc + 1) * 128, :], in_=ysb[par][:, :]
            ).then_inc(dst_sem.h, 16))
            st_done[oc] = dst_sem.take(16)
        return {"mm_done": mm_done, "evac_done": evac_done, "st_done": st_done}

    def dump_and_stop(src_dram, gate_sem, gate_val):
        wait("sync", (gate_sem, gate_val))
        b_sld, b_sty = sld.n, sty.n
        def mv(e, b_sld=b_sld, b_sty=b_sty):
            for dc in range(NCH):
                e.dma_start(out=xres[0][:, :],
                            in_=src_dram[dc * 128:(dc + 1) * 128, :]).then_inc(sld.h, 16)
                e.wait_ge(sld.h, b_sld + (dc + 1) * 16)
                e.dma_start(out=yT[dc * 128:(dc + 1) * 128, :],
                            in_=xres[0][:, :]).then_inc(sty.h, 16)
                e.wait_ge(sty.h, b_sty + (dc + 1) * 16)
        on("sync", mv)
        sld.take(16 * NCH)
        sty.take(16 * NCH)

    def finish():
        with nc.Block() as block:
            block.sync(lambda e: [f(e) for f in prog["sync"]])
            block.gpsimd(lambda e: [f(e) for f in prog["gpsimd"]])
            block.scalar(lambda e: [f(e) for f in prog["scalar"]])
            block.vector(lambda e: [f(e) for f in prog["vector"]])
            block.tensor(lambda e: [f(e) for f in prog["tensor"]])
        st.close()
        return nc

    if stop_phase == "gprep":
        wait("gpsimd", (dve, GT_READY))
        on("gpsimd", lambda e: e.dma_start(
            out=yT[0:160, :].rearrange("(r c) t -> r c t", c=4),
            in_=gt[0:40, :].rearrange("r (c t) -> r c t", c=4)).then_inc(sty.h, 16))
        wait("sync", (sty, sty.take(16)))
        on("sync", lambda e: e.nop())
        return finish()

    if stop_phase == "loads1":
        loads = []
        for dc in range(NCH):
            on("gpsimd", lambda e, dc=dc: e.dma_start(
                out=bigA[:, dc * T:(dc + 1) * T],
                in_=xT[dc * 128:(dc + 1) * 128, :]).then_inc(gld.h, 16))
            loads.append(gld.take(16))
        wait("gpsimd", (gld, loads[-1]))
        b_sty = sty.n
        def mvh0(e, b_sty=b_sty):
            for dc in range(NCH):
                e.dma_start(out=yT[dc * 128:(dc + 1) * 128, :],
                            in_=bigA[:, dc * T:(dc + 1) * T]).then_inc(sty.h, 16)
            e.wait_ge(sty.h, b_sty + NCH * 16)
        on("gpsimd", mvh0)
        sty.take(16 * NCH)
        return finish()

    if stop_phase in ("sq1", "mm1", "mmacc1"):
        loads = []
        for dc in range(NCH):
            on("gpsimd", lambda e, dc=dc: e.dma_start(
                out=bigA[:, dc * T:(dc + 1) * T],
                in_=xT[dc * 128:(dc + 1) * 128, :]).then_inc(gld.h, 16))
            loads.append(gld.take(16))
        if stop_phase == "sq1":
            sq_d = []
            for dc in range(NCH):
                par = dc % 2
                wait("scalar", (gld, loads[dc]))
                on("scalar", lambda e, par=par, dc=dc: e.activation(
                    xsq[par][:, :], bigA[:, dc * T:(dc + 1) * T], AF.Square
                ).then_inc(act.h, 1))
                sq_d.append(act.take(1))
            wait("gpsimd", (act, sq_d[-1]))
            b_sty = sty.n
            def mv1(e, b_sty=b_sty):
                e.dma_start(out=yT[0:128, :], in_=xsq[1][:, :]).then_inc(sty.h, 16)
                e.wait_ge(sty.h, b_sty + 16)
            on("gpsimd", mv1)
            sty.take(16)
            return finish()
        acc = stop_phase == "mmacc1"
        mms = []
        for dc in range(NCH):
            wait("tensor", (gld, loads[dc]), (dve, GT_READY))
            def mmz(e, dc=dc, acc=acc):
                for h2 in range(2):
                    last = e.matmul(psB[0:1, h2 * 512:(h2 + 1) * 512], onec[:, :],
                             bigA[:, dc * T + h2 * 512: dc * T + (h2 + 1) * 512],
                             start=(dc == 0 if acc else True),
                             stop=(dc == NCH - 1 if acc else True),
                             skip_group_check=True)
                last.then_inc(pe.h, 1)
            on("tensor", mmz)
            mms.append(pe.take(1))
        wait("vector", (pe, mms[-1]))
        on("vector", lambda e: e.tensor_copy(fb[0:1, 0:1024].bitcast(F32), psB[0:1, 0:1024]).then_inc(dve.h, 1))
        sd = dve.take(1)
        wait("sync", (dve, sd))
        b_sty = sty.n
        def mv2(e, b_sty=b_sty):
            e.dma_start(out=yT[0:1, 0:1024], in_=fb[0:1, 0:1024]).then_inc(sty.h, 16)
            e.wait_ge(sty.h, b_sty + 16)
        on("sync", mv2)
        sty.take(16)
        return finish()

    if stop_phase == "stats1":
        loads = []
        for dc in range(NCH):
            on("gpsimd", lambda e, dc=dc: e.dma_start(
                out=bigA[:, dc * T:(dc + 1) * T],
                in_=xT[dc * 128:(dc + 1) * 128, :]).then_inc(gld.h, 16))
            loads.append(gld.take(16))
        stats_mm = []
        for dc in range(NCH):
            par = dc % 2
            wait("scalar", (gld, loads[dc]))
            if dc >= 2:
                wait("scalar", (pe, stats_mm[dc - 2]))
            def sq_em(e, par=par, dc=dc):
                e.activation(xsq[par][:, :], bigA[:, dc * T:(dc + 1) * T], AF.Square)
                e.activation(small[:, 56:60], xsq[par][:, 1020:1024],
                             AF.Identity).then_inc(act.h, 1)
            on("scalar", sq_em)
            sqd = act.take(1)
            wait("tensor", (act, sqd))
            def smmz(e, dc=dc, par=par):
                for h2 in range(2):
                    e.matmul(psB[0:1, h2 * 512:(h2 + 1) * 512], onec[:, :],
                             bigA[:, dc * T + h2 * 512: dc * T + (h2 + 1) * 512],
                             start=(dc == 0), stop=(dc == NCH - 1), skip_group_check=True)
                    last = e.matmul(psB[0:1, 1024 + h2 * 512: 1024 + (h2 + 1) * 512],
                             onec[:, :], xsq[par][:, h2 * 512:(h2 + 1) * 512],
                             start=(dc == 0), stop=(dc == NCH - 1), skip_group_check=True)
                last.then_inc(pe.h, 1)
            on("tensor", smmz)
            stats_mm.append(pe.take(1))
        wait("vector", (pe, stats_mm[-1]), (dve, GPREP))
        on("vector", lambda e: e.tensor_copy(fb[0:1, 0:2048].bitcast(F32), psB[0:1, 0:2048]).then_inc(dve.h, 1))
        sd = dve.take(1)
        wait("sync", (dve, sd))
        b_sty = sty.n
        def mvs(e, b_sty=b_sty):
            e.dma_start(out=yT[0:1, 0:1024], in_=fb[0:1, 0:1024]).then_inc(sty.h, 16)
            e.dma_start(out=yT[1:2, 0:1024], in_=fb[0:1, 1024:2048]).then_inc(sty.h, 16)
            e.wait_ge(sty.h, b_sty + 32)
        on("sync", mvs)
        sty.take(32)
        return finish()

    if stop_phase == "hat1":
        msx = ln_and_hat(xT)
        wait("gpsimd", (dve, msx["hat"][-1]))
        b_sty = sty.n
        def mvh(e, b_sty=b_sty):
            for dc in range(NCH):
                e.dma_start(out=yT[dc * 128:(dc + 1) * 128, :],
                            in_=bigA[:, dc * T:(dc + 1) * T]).then_inc(sty.h, 16)
            e.wait_ge(sty.h, b_sty + NCH * 16)
        on("gpsimd", mvh)
        sty.take(16 * NCH)
        return finish()

    if stop_phase == "wpan1":
        wait("gpsimd", (dve, GT_READY))
        on("gpsimd", lambda e: e.dma_start(out=pan[0][:, :], in_=wb[0, :, :]).then_inc(gld.h, 16))
        wl = gld.take(16)
        b_sty = sty.n
        def mvw(e, b_sty=b_sty, wl=wl):
            e.wait_ge(gld.h, wl)
            for dc in range(4):
                e.dma_start(out=yT[dc * 128:(dc + 1) * 128, 0:128],
                            in_=pan[0][:, dc * 128:(dc + 1) * 128]).then_inc(sty.h, 16)
            e.wait_ge(sty.h, b_sty + 4 * 16)
        on("gpsimd", mvw)
        sty.take(16 * 4)
        return finish()

    f1 = ffn(xT, x1, stx, xres_extra=((dve, GPREP),))
    FFN1_STORES = stx.n
    if stop_phase == "ffn1":
        dump_and_stop(x1[:, :], stx, FFN1_STORES)
        return finish()

    wait("gpsimd", (pe, f1["mm_done"][-1]))
    ms2 = ln_and_hat(x1, store_gate=(stx, FFN1_STORES))
    hat2_all = ms2["hat"][-1]

    qmm_done = [0] * 96
    qev_done = [0] * 96
    qst_done = [0] * 96
    vtr_done = {}
    kn_red = {}
    kn_mm = {}
    QKV_ORDER = list(range(32, 96)) + list(range(32))
    for pos, oc in enumerate(QKV_ORDER):
        par = pos % 2
        prev2 = QKV_ORDER[pos - 2] if pos >= 2 else None
        if pos >= 2:
            wait("gpsimd", (pe, qmm_done[prev2]))
        def panld2(e, oc=oc, par=par):
            e.dma_start(out=pan[par][:, :], in_=wqkv[oc, :, :]).then_inc(gld.h, 16)
            e.dma_start(out=pan[par][:, 3968:4096],
                        in_=wqkv[oc, :, 3968:4096]).then_inc(gld.h, 16)
        on("gpsimd", panld2)
        wl = gld.take(32)
        wait("tensor", (gld, wl), (dve, hat2_all))
        if pos >= 2:
            wait("tensor", (act, qev_done[prev2]))
        def qmm(e, oc=oc, par=par):
            po = psA[:, par * 1024:(par + 1) * 1024]
            for dc in range(NCH):
                for h2 in range(2):
                    last = e.matmul(po[:, h2 * 512:(h2 + 1) * 512],
                             pan[par][:, dc * 128:(dc + 1) * 128],
                             bigA[:, dc * T + h2 * 512: dc * T + (h2 + 1) * 512],
                             start=(dc == 0), stop=(dc == NCH - 1), skip_group_check=True)
            last.then_inc(pe.h, 1)
        on("tensor", qmm)
        qmm_done[oc] = pe.take(1)
        w = [(pe, qmm_done[oc])]
        if pos >= 2:
            w.append((stx, qst_done[prev2]))
            if prev2 >= 64:
                w.append((tld, vtr_done[prev2]))
            if 32 <= prev2 < 64:
                w.append((pe, kn_mm[prev2]))
        wait("scalar", *w)
        def qev(e, oc=oc, par=par):
            if oc < 32:
                e.activation(stag[par][:, :], psA[:, par * 1024:(par + 1) * 1024],
                             AF.Identity, bias=biasqs[:, oc:oc + 1], scale=QSCALE)
            else:
                e.activation(stag[par][:, :], psA[:, par * 1024:(par + 1) * 1024],
                             AF.Identity, bias=biasq[:, oc:oc + 1])
            e.activation(small[:, 56:60], stag[par][:, 1020:1024],
                         AF.Identity).then_inc(act.h, 1)
        on("scalar", qev)
        qev_done[oc] = act.take(1)
        if oc < 32:
            wait("sync", (act, qev_done[oc]))
            on("sync", lambda e, oc=oc, par=par: e.dma_start(
                out=qspill[oc, :, :], in_=stag[par][:, :]).then_inc(stx.h, 16))
            qst_done[oc] = stx.take(16)
        elif oc < 64:
            kc = oc - 32
            wait("sync", (act, qev_done[oc]))
            on("sync", lambda e, kc=kc, par=par: e.dma_start(
                out=kv_in.ap()[kc * 128:(kc + 1) * 128, :], in_=stag[par][:, :]
            ).then_inc(stx.h, 16))
            qst_done[oc] = stx.take(16)
            prev_kn = kn_mm.get(oc - 1)
            if prev_kn:
                wait("scalar", (pe, prev_kn))
            def ksq_em(e, par=par):
                e.activation(ksq[:, :], stag[par][:, :], AF.Square)
                e.activation(small[:, 56:60], ksq[:, 1020:1024],
                             AF.Identity).then_inc(act.h, 1)
            on("scalar", ksq_em)
            ksd = act.take(1)
            w2 = [(act, ksd)]
            if kc >= 1:
                w2.append((dve, kn_red[oc - 1]))
            wait("tensor", *w2)
            def knmm(e):
                for h2 in range(2):
                    e.matmul(psB[0:1, 1024 + h2 * 512: 1024 + (h2 + 1) * 512],
                             onec[:, :], ksq[:, h2 * 512:(h2 + 1) * 512],
                             start=True, stop=True, skip_group_check=True)
                e.matmul(psB[0:1, 512:513], onec[:, :], onec[:, :], start=True,
                         stop=True, skip_group_check=True)
                last = e.matmul(psB[0:1, 512:513], onec[:, :], onec[:, :], start=True,
                                stop=True, skip_group_check=True)
                last.then_inc(pe.h, 1)
            on("tensor", knmm)
            kn_mm[oc] = pe.take(1)
            wait("vector", (pe, kn_mm[oc]))
            vk = dve.take(1)
            def knr(e, kc=kc, vk=vk):
                e.reduce_max(kred, psB[0:1, 1024:2048], AXX).then_inc(dve.h, 1)
                e.wait_ge(dve.h, vk)
                if kc == 0:
                    e.tensor_copy(kmx, kred).then_inc(dve.h, 1)
                else:
                    e.tensor_tensor(kmx, kmx, kred, ALU.max).then_inc(dve.h, 1)
            on("vector", knr)
            kn_red[oc] = dve.take(1)
        else:
            vc = oc - 64
            wait("scalar", (act, qev_done[oc]))
            if pos >= 2 and prev2 >= 64:
                wait("scalar", (stx, qst_done[prev2]))
            def vtr(e, par=par):
                src = stag[par][:, :].rearrange("d (tc t) -> d tc t", tc=8)
                dst = vstg[par][:, :].rearrange("t (tc d) -> t tc d", tc=8)
                e.dma_start(out=dst, in_=src, transpose=True).then_inc(tld.h, 16)
            on("scalar", vtr)
            vtr_done[oc] = tld.take(16)
            wait("sync", (tld, vtr_done[oc]))
            on("sync", lambda e, vc=vc, par=par: e.dma_start(
                out=kv_in.ap()[4096 + vc * 128: 4096 + (vc + 1) * 128, :],
                in_=vstg[par][:, :]).then_inc(stx.h, 16))
            qst_done[oc] = stx.take(16)
        if pos == 63:
            KV_STORES = stx.n
            KN_LAST = kn_red[63]
            wait("sync", (dve, KN_LAST))
            on("sync", lambda e: e.dma_start(out=km_in.ap()[:, :],
                                             in_=kmx).then_inc(stx.h, 16))
            KM_STORE = stx.take(16)
            wait("gpsimd", (stx, KV_STORES))
            on("gpsimd", lambda e: e.collective_compute(
                "AllGather", ALU.bypass,
                replica_groups=[[0, 1, 2, 3], [4, 5, 6, 7]],
                ins=[kv_in.ap().opt()], outs=[kv_out.ap().opt()]).then_inc(cc1.h))
            cc1.take(1)
            wait("gpsimd", (stx, KM_STORE))
            on("gpsimd", lambda e: e.collective_compute(
                "AllGather", ALU.bypass,
                replica_groups=[[0, 1, 2, 3, 4, 5, 6, 7]],
                ins=[km_in.ap().opt()], outs=[km_out.ap().opt()]).then_inc(cc2.h))
            cc2.take(1)
    QKV_STORES = stx.n

    wait("sync", (cc2, 1))
    on("sync", lambda e: e.dma_start(
        out=km8, in_=km_out.ap().rearrange("r one -> one r")).then_inc(sld.h, 16))
    kml = sld.take(16)
    wait("vector", (sld, kml))
    on("vector", lambda e: e.reduce_max(kmg, km8, AXX).then_inc(dve.h, 1))
    kmr = dve.take(1)
    wait("tensor", (dve, kmr), (act, qev_done[95]))
    def kmbc(e):
        e.matmul(psB[:, 0:1], oner[:, :], kmg, start=True, stop=True,
                 skip_group_check=True)
        e.matmul(psB[0:1, 512:513], onec[:, :], onec[:, :], start=True, stop=True,
                 skip_group_check=True)
        e.matmul(psB[0:1, 512:513], onec[:, :], onec[:, :], start=True, stop=True,
                 skip_group_check=True).then_inc(pe.h, 1)
    on("tensor", kmbc)
    kmb = pe.take(1)
    wait("vector", (pe, kmb))
    on("vector", lambda e: e.tensor_copy(kmax2c, psB[:, 0:1]).then_inc(dve.h, 1))
    KMAXC = dve.take(1)
    if stop_phase == "qkv":
        wait("gpsimd", (cc1, 1), (cc2, 1))
        b_gld, b_sty = gld.n, sty.n
        def mvq(e, b_gld=b_gld, b_sty=b_sty):
            for dc in range(NCH):
                e.dma_start(out=xres[0][:, :],
                            in_=kv_in.ap()[dc * 128:(dc + 1) * 128, :]).then_inc(gld.h, 16)
                e.wait_ge(gld.h, b_gld + (dc + 1) * 16)
                e.dma_start(out=yT[dc * 128:(dc + 1) * 128, :],
                            in_=xres[0][:, :]).then_inc(sty.h, 16)
                e.wait_ge(sty.h, b_sty + (dc + 1) * 16)
        on("gpsimd", mvq)
        gld.take(16 * NCH)
        sty.take(16 * NCH)
        return finish()

    if stop_phase == "attq":
        wait("gpsimd", (dve, KMAXC))
        b_sty = sty.n
        def mvq2(e, b_sty=b_sty):
            for hh in range(4):
                e.dma_start(out=yT[hh * 128:(hh + 1) * 128, :],
                            in_=qspill[hh, :, :]).then_inc(sty.h, 16)
            e.dma_start(out=yT[512:513, 0:1], in_=kmg).then_inc(sty.h, 16)
            e.dma_start(out=yT[513:514, 0:8], in_=km8).then_inc(sty.h, 16)
            e.dma_start(out=yT[514:515, 0:1], in_=kmx).then_inc(sty.h, 16)
            e.wait_ge(sty.h, b_sty + 16 * 7)
        on("gpsimd", mvq2)
        sty.take(16 * 7)
        return finish()

    k_ld = [0] * NH
    v_ld = [0] * NH
    q_ld = [0] * NH
    sc_done = [[0] * 8 for _ in range(NH)]
    ex_done = [[0] * 8 for _ in range(NH)]
    nm_done = [[0] * 8 for _ in range(NH)]
    tr_done = [[0] * 8 for _ in range(NH)]
    ctx_done = [0] * NH
    ctxev = [0] * NH

    def kvq_load(h):
        if h == 0:
            wait("sync", (cc1, 1))
        else:
            wait("sync", (pe, sc_done[h - 1][7]))
        def kl(e, h=h):
            for r in range(4):
                e.dma_start(out=kbuf[:, r * 1024:(r + 1) * 1024],
                            in_=kv_out.ap()[r * 8192 + h * 128: r * 8192 + (h + 1) * 128, :]
                            ).then_inc(sld.h, 16)
            e.dma_start(out=kbuf[:, 3 * 1024 + 896: 4 * 1024],
                        in_=kv_out.ap()[3 * 8192 + h * 128: 3 * 8192 + (h + 1) * 128, 896:1024]
                        ).then_inc(sld.h, 16)
        on("sync", kl)
        k_ld[h] = sld.take(80)
        if h > 0:
            wait("sync", (pe, ctx_done[h - 1]))
        def vl(e, h=h):
            for r in range(4):
                e.dma_start(out=vbuf[:, r * 1024:(r + 1) * 1024],
                            in_=kv_out.ap()[r * 8192 + 4096 + h * 128:
                                            r * 8192 + 4096 + (h + 1) * 128, :]
                            ).then_inc(sld.h, 16)
            e.dma_start(out=vbuf[:, 3 * 1024 + 896: 4 * 1024],
                        in_=kv_out.ap()[3 * 8192 + 4096 + h * 128:
                                        3 * 8192 + 4096 + (h + 1) * 128, 896:1024]
                        ).then_inc(sld.h, 16)
        on("sync", vl)
        v_ld[h] = sld.take(80)
        qpar = h % 2
        def ql(e, h=h, qpar=qpar):
            e.dma_start(out=qh[:, qpar * T:(qpar + 1) * T],
                        in_=qspill[h, :, :]).then_inc(sld.h, 16)
            e.dma_start(out=qh[:, qpar * T + 896: qpar * T + 1024],
                        in_=qspill[h, :, 896:1024]).then_inc(sld.h, 16)
        on("sync", ql)
        q_ld[h] = sld.take(32)

    NH_RUN = 1 if stop_phase == "attn1" else NH
    kvq_load(0)
    for h in range(NH_RUN):
        qpar = h % 2
        qhv = qh[:, qpar * T:(qpar + 1) * T]
        w = [(sld, q_ld[h])]
        if h >= 1:
            w.append((pe, ctx_done[h - 1]))
        wait("scalar", *w)
        def qsq_em(e, qhv=qhv):
            e.activation(qsq[:, :], qhv, AF.Square)
            e.activation(small[:, 56:60], qsq[:, 1020:1024],
                         AF.Identity).then_inc(act.h, 1)
        on("scalar", qsq_em)
        qsqd = act.take(1)
        w = [(act, qsqd)]
        if h >= 1:
            w.append((act, ex_done[h - 1][7]))
            w.append((dve, ctxev[h - 1]))
        wait("tensor", *w)
        def qnmm(e):
            for qt in range(8):
                e.matmul(psA[:, qt:qt + 1], qsq[:, qt * 128:(qt + 1) * 128],
                         onec[:, :], start=True, stop=True, skip_group_check=True)
            e.matmul(psA[0:1, 8:9], onec[:, :], onec[:, :], start=True, stop=True,
                     skip_group_check=True)
            last = e.matmul(psA[0:1, 8:9], onec[:, :], onec[:, :], start=True,
                            stop=True, skip_group_check=True)
            last.then_inc(pe.h, 1)
        on("tensor", qnmm)
        qn = pe.take(1)
        wait("vector", (pe, qn), (dve, KMAXC))
        on("vector", lambda e: e.tensor_scalar(csq, psA[:, 0:8], kmax2c, None,
                                               ALU.mult).then_inc(dve.h, 1))
        cs = dve.take(1)
        wait("scalar", (dve, cs))
        on("scalar", lambda e: e.activation(ccol, csq, AF.Sqrt,
                                            scale=MARGIN2).then_inc(act.h, 1))
        cd = act.take(1)
        wait("vector", (act, cd))
        on("vector", lambda e: e.tensor_scalar(negc, ccol, -1.0, None,
                                               ALU.mult).then_inc(dve.h, 1))
        ncd = dve.take(1)

        nm_emit = []
        for qt in range(8):
            ppar = qt % 2
            pv = pbuf[:, ppar * 4096:(ppar + 1) * 4096]
            w = [(sld, k_ld[h]), (dve, ncd)]
            if qt >= 1:
                w.append((act, ex_done[h][qt - 1]))
            elif h >= 1:
                w.append((act, ex_done[h - 1][7]))
                w.append((dve, ctxev[h - 1]))
            wait("tensor", *w)
            def smm2(e, qhv=qhv, qt=qt):
                for ks in range(8):
                    ps = (psA if ks < 4 else psB)[:, (ks % 4) * 512:(ks % 4 + 1) * 512]
                    last = e.matmul(ps, qhv[:, qt * 128:(qt + 1) * 128],
                             kbuf[:, ks * 512:(ks + 1) * 512],
                             start=True, stop=True, skip_group_check=True)
                last.then_inc(pe.h, 1)
            on("tensor", smm2)
            sc_done[h][qt] = pe.take(1)
            w = [(pe, sc_done[h][qt])]
            if qt >= 1:
                w.append((dve, nm_done[h][qt - 1]))
            elif h >= 1:
                w.append((dve, nm_done[h - 1][7]))
            if qt >= 1:
                w.append((tld, tr_done[h][qt - 1]))
            elif h >= 1:
                w.append((tld, tr_done[h - 1][7]))
            wait("scalar", *w)
            def expp(e, pv=pv, qt=qt):
                e.activation(pv[:, 0:2048], psA[:, :], AF.Exp,
                             bias=negc[:, qt:qt + 1], accum_out=sumab[:, 0:1])
                e.activation(pv[:, 2048:4096], psB[:, :], AF.Exp,
                             bias=negc[:, qt:qt + 1],
                             accum_out=sumab[:, 1:2]).then_inc(act.h, 1)
            on("scalar", expp)
            ex_done[h][qt] = act.take(1)
            wait("vector", (act, ex_done[h][qt]))
            v1 = dve.take(1)
            v2 = dve.take(1)
            def pnorm(e, pv=pv, v1=v1, v2=v2):
                e.tensor_tensor(sumab[:, 2:3], sumab[:, 0:1], sumab[:, 1:2],
                                ALU.add).then_inc(dve.h, 1)
                e.wait_ge(dve.h, v1)
                e.reciprocal(sumab[:, 3:4], sumab[:, 2:3]).then_inc(dve.h, 1)
                e.wait_ge(dve.h, v2)
                e.tensor_scalar(pv[:, :], pv[:, :], sumab[:, 3:4], None,
                                ALU.mult).then_inc(dve.h, 1)
            on("vector", pnorm)
            nm_done[h][qt] = dve.take(1)
            w = [(dve, nm_done[h][qt])]
            if h >= 1 and qt == 0:
                w.append((pe, ctx_done[h - 1]))
            wait("scalar", *w)
            def ptr(e, pv=pv, qt=qt):
                src = pv.rearrange("q (kc k) -> q kc k", kc=32)
                dst = bigB[:, :].rearrange("k (kc qq) -> k kc qq", kc=32)[:, :, qt * 128:(qt + 1) * 128]
                e.dma_start(out=dst, in_=src, transpose=True).then_inc(tld.h, 16)
            on("scalar", ptr)
            tr_done[h][qt] = tld.take(16)

        wait("tensor", (tld, tr_done[h][7]), (sld, v_ld[h]), (act, ex_done[h][7]))
        def cmm(e):
            for kc in range(32):
                for h2 in range(2):
                    e.matmul(psA[:, h2 * 512:(h2 + 1) * 512],
                             vbuf[:, kc * 128:(kc + 1) * 128],
                             bigB[:, kc * 1024 + h2 * 512: kc * 1024 + (h2 + 1) * 512],
                             start=(kc == 0), stop=(kc == 31), skip_group_check=True)
            e.matmul(psB[0:1, 0:1], onec[:, :], onec[:, :], start=True, stop=True,
                     skip_group_check=True)
            last = e.matmul(psB[0:1, 0:1], onec[:, :], onec[:, :], start=True,
                            stop=True, skip_group_check=True)
            last.then_inc(pe.h, 1)
        on("tensor", cmm)
        ctx_done[h] = pe.take(1)
        wait("vector", (pe, ctx_done[h]))
        def cev(e, h=h):
            e.tensor_copy(bigA[:, h * T: h * T + 512], psA[:, 0:512])
            e.tensor_copy(bigA[:, h * T + 512:(h + 1) * T], psA[:, 512:1024])
            e.tensor_copy(small[:, 56:60], bigA[:, h * T + 1020: h * T + 1024]
                          ).then_inc(dve.h, 1)
        on("vector", cev)
        ctxev[h] = dve.take(1)
        if h + 1 < NH_RUN:
            kvq_load(h + 1)

    if stop_phase == "attn1":
        wait("gpsimd", (dve, ctxev[0]))
        b_sty = sty.n
        def mvp(e, b_sty=b_sty):
            for cc in range(NCH):
                e.dma_start(out=yT[cc * 128:(cc + 1) * 128, :],
                            in_=bigB[:, cc * T:(cc + 1) * T]).then_inc(sty.h, 16)
            e.wait_ge(sty.h, b_sty + 16 * NCH)
        on("gpsimd", mvp)
        sty.take(16 * NCH)
        return finish()

    if stop_phase == "attn":
        wait("gpsimd", (dve, ctxev[NH - 1]))
        b_sty = sty.n
        def mva(e, b_sty=b_sty):
            for hh in range(NH):
                e.dma_start(out=yT[hh * 128:(hh + 1) * 128, :],
                            in_=bigA[:, hh * T:(hh + 1) * T]).then_inc(sty.h, 16)
            e.wait_ge(sty.h, b_sty + 16 * NH)
        on("gpsimd", mva)
        sty.take(16 * NH)
        return finish()

    omm_done = [0] * NCH
    oev_done = [0] * NCH
    ost_done = [0] * NCH
    for oc in range(NCH):
        par = oc % 2
        if oc >= 2:
            wait("gpsimd", (pe, omm_done[oc - 2]))
        else:
            wait("gpsimd", (pe, ctx_done[NH - 1]))
        def panld3(e, oc=oc, par=par):
            e.dma_start(out=pan[par][:, :], in_=wo[oc, :, :]).then_inc(gld.h, 16)
            e.dma_start(out=pan[par][:, 3968:4096],
                        in_=wo[oc, :, 3968:4096]).then_inc(gld.h, 16)
        on("gpsimd", panld3)
        wl = gld.take(32)
        if oc >= 2:
            wait("sync", (dve, oev_done[oc - 2]))
        else:
            wait("sync", (dve, f1["evac_done"][-1]))
        def xrl2(e, oc=oc, par=par):
            e.dma_start(out=xres[par][:, :],
                        in_=x1[oc * 128:(oc + 1) * 128, :]).then_inc(sld.h, 16)
            e.dma_start(out=xres[par][:, 896:1024],
                        in_=x1[oc * 128:(oc + 1) * 128, 896:1024]).then_inc(sld.h, 16)
        on("sync", xrl2)
        xl = sld.take(32)
        wait("tensor", (gld, wl), (dve, ctxev[NH - 1]))
        if oc >= 2:
            wait("tensor", (dve, oev_done[oc - 2]))
        def omm(e, oc=oc, par=par):
            po = psB[:, par * 1024:(par + 1) * 1024]
            for dc in range(NCH):
                for h2 in range(2):
                    last = e.matmul(po[:, h2 * 512:(h2 + 1) * 512],
                             pan[par][:, dc * 128:(dc + 1) * 128],
                             bigA[:, dc * T + h2 * 512: dc * T + (h2 + 1) * 512],
                             start=(dc == 0), stop=(dc == NCH - 1), skip_group_check=True)
            last.then_inc(pe.h, 1)
        on("tensor", omm)
        omm_done[oc] = pe.take(1)
        if oc >= 2:
            wait("vector", (pe, omm_done[oc]), (sld, xl), (stx, ost_done[oc - 2]))
        else:
            wait("vector", (pe, omm_done[oc]), (sld, xl))
        def oev(e, oc=oc, par=par):
            po = psB[:, par * 1024:(par + 1) * 1024]
            e.scalar_tensor_tensor(ysb[par][:, :], po, biaso[:, oc:oc + 1],
                                   xres[par][:, :], ALU.add, ALU.add)
            e.tensor_copy(small[:, 56:60], ysb[par][:, 1020:1024]).then_inc(dve.h, 1)
        on("vector", oev)
        oev_done[oc] = dve.take(1)
        wait("sync", (dve, oev_done[oc]))
        on("sync", lambda e, oc=oc, par=par: e.dma_start(
            out=x2[oc * 128:(oc + 1) * 128, :], in_=ysb[par][:, :]).then_inc(stx.h, 16))
        ost_done[oc] = stx.take(16)
    OUTPROJ_STORES = stx.n
    if stop_phase == "proj":
        dump_and_stop(x2[:, :], stx, OUTPROJ_STORES)
        return finish()

    wait("gpsimd", (pe, omm_done[-1]))
    GT_READY = oev_done[-1]
    r7 = ffn(x2, yT, sty, store_gate=(stx, OUTPROJ_STORES),
             xres_extra=((dve, oev_done[-1]),),
             stop_at_hat=(stop_phase == "hat3"),
             dump_dbg=(stop_phase == "fulldbg"))
    if stop_phase == "hat3":
        return finish()
    Y_STORES = sty.n
    if stop_phase == "fulldbg":
        wait("sync", (sty, Y_STORES))
        b_sld2, b_sty2 = sld.n, sty.n
        def mvx2(e, b_sld2=b_sld2, b_sty2=b_sty2):
            for dc in range(NCH):
                e.dma_start(out=xres[0][:, :],
                            in_=x2[dc * 128:(dc + 1) * 128, :]).then_inc(sld.h, 16)
                e.wait_ge(sld.h, b_sld2 + (dc + 1) * 16)
                e.dma_start(out=dbg2[dc * 128:(dc + 1) * 128, :],
                            in_=xres[0][:, :]).then_inc(sty.h, 16)
                e.wait_ge(sty.h, b_sty2 + (dc + 1) * 16)
        on("sync", mvx2)
        sld.take(16 * NCH)
        sty.take(16 * NCH)

    wait("sync", (sty, Y_STORES))
    on("sync", lambda e: e.nop())
    return finish()


_M16, _SVEC = None, None
_NC_CACHE = None


def _host_prep(inputs):
    global _M16, _SVEC
    if _M16 is None:
        _M16, _SVEC = _m16_table()
    x = np.ascontiguousarray(inputs["L0_attn_x"], np.float32).reshape(8192, 4096)
    qkvw = np.ascontiguousarray(inputs["L0_attn_attn_qkvw"], np.float32)
    qkvb = np.ascontiguousarray(inputs["L0_attn_attn_qkvb"], np.float32)
    ow = np.ascontiguousarray(inputs["L0_attn_attn_ow"], np.float32)
    ob = np.ascontiguousarray(inputs["L0_attn_attn_ob"], np.float32)
    Wb = np.ascontiguousarray(inputs["L0_ffn_W_base"], np.float32)
    bb = np.ascontiguousarray(inputs["L0_ffn_b_base"], np.float32)
    enc0 = np.ascontiguousarray(inputs["L0_ffn_enc0"], np.float32)
    enc1 = np.ascontiguousarray(inputs["L0_ffn_enc1"], np.float32)
    c0 = np.ascontiguousarray(inputs["L0_ffn_coefs0"], np.float32)
    c1 = np.ascontiguousarray(inputs["L0_ffn_coefs1"], np.float32)
    m0 = inputs["L0_ffn_mask0"]
    m1 = inputs["L0_ffn_mask1"]

    wqkv_r = np.ascontiguousarray(
        qkvw.reshape(32, 128, 96, 128).transpose(2, 1, 0, 3)).reshape(96, 128, 4096)
    wb_r = np.ascontiguousarray(
        Wb.reshape(32, 128, 32, 128).transpose(0, 3, 2, 1)).reshape(32, 128, 4096)
    wo_r = np.ascontiguousarray(
        ow.reshape(32, 128, 32, 128).transpose(2, 1, 0, 3)).reshape(32, 128, 4096)
    enc = np.concatenate([enc0, enc1], 0)
    enc_r = np.ascontiguousarray(
        enc.reshape(40, 32, 128).transpose(2, 1, 0)).reshape(128, 32 * 40)
    cfull = np.concatenate(
        [c0.reshape(3, 32, 128, 32), c1.reshape(3, 32, 128, 8)], -1)
    coef_r = np.ascontiguousarray(cfull.transpose(2, 1, 0, 3)).reshape(128, 32 * 3 * 40)
    mfull = np.concatenate(
        [m0.reshape(3, 32, 128, 32), m1.reshape(3, 32, 128, 8)], -1)
    mask_r = np.ascontiguousarray(
        mfull.transpose(2, 1, 0, 3)).reshape(128, 32 * 3 * 40).view(np.uint8)
    qkvb_c = np.ascontiguousarray(qkvb.reshape(96, 128).T)
    bb_c = np.ascontiguousarray(bb.reshape(32, 128).T)
    ob_c = np.ascontiguousarray(ob.reshape(32, 128).T)

    shared = {"wqkv": wqkv_r, "wb": wb_r, "wo": wo_r, "m16": _M16,
              "encr": enc_r, "coefr": coef_r, "maskr": mask_r, "svec": _SVEC,
              "qkvb_c": qkvb_c, "bb_c": bb_c, "ob_c": ob_c}
    in_maps = []
    for c in range(8):
        d = dict(shared)
        d["xT"] = np.ascontiguousarray(x[c * 1024:(c + 1) * 1024, :].T)
        in_maps.append(d)
    return in_maps


def kernel(**inputs):
    global _NC_CACHE
    in_maps = _host_prep(inputs)
    if _NC_CACHE is None:
        _NC_CACHE = build_nc()
    res = run_bass_kernel_spmd(_NC_CACHE, in_maps, core_ids=list(range(8)))
    parts = [np.ascontiguousarray(np.asarray(r["yT"]).T) for r in res.results]
    return np.concatenate(parts, 0).reshape(2, 4096, 4096).astype(np.float32)


# version 04 (final)
# speedup vs baseline: 1.0033x; 1.0033x over previous
import math
from contextlib import ExitStack

import numpy as np
import concourse.bass as bass
import concourse.mybir as mybir
from concourse.bass_utils import run_bass_kernel_spmd

F32 = mybir.dt.float32
BF16 = mybir.dt.bfloat16
F16 = mybir.dt.float16
U8 = mybir.dt.uint8
AF = mybir.ActivationFunctionType
ALU = mybir.AluOpType
AXX = mybir.AxisListType.X

T = 1024
D = 4096
NCH = 32
NH = 32
QSCALE = 1.0 / math.sqrt(128.0)
MARGIN2 = 1.05
EPS = 1e-5
INV_D = 1.0 / 4096.0
PI_OVER = math.pi / 2048.0


def _m16_table():
    n = 4096
    t = np.arange(n, dtype=np.int64)
    M = np.zeros((n, n), np.int64)
    M[:, 0] = 1024
    f = np.arange(1, (n - 1) // 2 + 1, dtype=np.int64)
    ft = t[:, None] * f[None, :]
    M[:, 2 * f - 1] = ft + 1024
    M[:, 2 * f] = ft
    M[:, n - 1] = 2048 * t + 1024
    M = ((M + 2048) % 4096) - 2048
    MT = np.ascontiguousarray(M.T).astype(np.float16)
    s = np.full(n, math.sqrt(2.0 / n), np.float64)
    s[0] = 1.0 / math.sqrt(n)
    s[n - 1] = 1.0 / math.sqrt(n)
    svec = np.ascontiguousarray((0.5 * s).reshape(32, 128).T).astype(np.float32)
    return MT.reshape(32, 128, 4096), svec


class Sem:
    def __init__(self, stack, nc, name):
        self.h = stack.enter_context(nc.semaphore(name))
        self.n = 0

    def take(self, k):
        self.n += k
        return self.n


def build_nc(stop_phase=None):
    nc = bass.Bass("TRN2", target_bir_lowering=False, detect_race_conditions=False)

    xT = nc.dram_tensor("xT", [D, T], F32, kind="ExternalInput").ap()
    wqkv = nc.dram_tensor("wqkv", [96, 128, NCH * 128], F32, kind="ExternalInput").ap()
    wb = nc.dram_tensor("wb", [32, 128, NCH * 128], F32, kind="ExternalInput").ap()
    wo = nc.dram_tensor("wo", [32, 128, NCH * 128], F32, kind="ExternalInput").ap()
    m16 = nc.dram_tensor("m16", [32, 128, 4096], F16, kind="ExternalInput").ap()
    encr = nc.dram_tensor("encr", [128, NCH * 40], F32, kind="ExternalInput").ap()
    coefr = nc.dram_tensor("coefr", [128, NCH * 3 * 40], F32, kind="ExternalInput").ap()
    maskr = nc.dram_tensor("maskr", [128, NCH * 3 * 40], U8, kind="ExternalInput").ap()
    svec_in = nc.dram_tensor("svec", [128, 32], F32, kind="ExternalInput").ap()
    qkvb_c = nc.dram_tensor("qkvb_c", [128, 96], F32, kind="ExternalInput").ap()
    bb_c = nc.dram_tensor("bb_c", [128, 32], F32, kind="ExternalInput").ap()
    ob_c = nc.dram_tensor("ob_c", [128, 32], F32, kind="ExternalInput").ap()
    yT = nc.dram_tensor("yT", [D, T], F32, kind="ExternalOutput").ap()
    if stop_phase == "fulldbg":
        dbg = nc.dram_tensor("dbg", [D, T], F32, kind="ExternalOutput").ap()
        dbg2 = nc.dram_tensor("dbg2", [D, T], F32, kind="ExternalOutput").ap()
        dbg3 = nc.dram_tensor("dbg3", [132, T], F32, kind="ExternalOutput").ap()

    x1 = nc.dram_tensor("x1", [D, T], F32).ap()
    x2 = nc.dram_tensor("x2", [D, T], F32).ap()
    qspill = nc.dram_tensor("qspill", [32, 128, T], BF16).ap()
    kv_in = nc.dram_tensor("kv_in", [8192, T], BF16)
    kv_out = nc.dram_tensor("kv_out", [4 * 8192, T], BF16)
    km_in = nc.dram_tensor("km_in", [1, 1], F32)
    km_out = nc.dram_tensor("km_out", [8, 1], F32)

    st = ExitStack()
    sbt = lambda name, shape, dt: st.enter_context(nc.sbuf_tensor(name, shape, dt))

    bigA = sbt("bigA", [128, NCH * T], BF16)
    bigB = sbt("bigB", [128, NCH * T], BF16)
    fb = sbt("fb", [128, 7680], F32)
    pbuf = sbt("pbuf", [128, 2 * 4096], BF16)
    kvb = sbt("kvb", [128, 2 * 4096], BF16)
    qh = sbt("qh", [128, 2 * T], BF16)
    gt = sbt("gt", [40, 4096], BF16)
    encs = sbt("encs", [128, NCH * 40], BF16)
    onec = sbt("onec", [128, 1], BF16)
    oner = sbt("oner", [1, 128], F32)
    biasq = sbt("biasq", [128, 96], F32)
    biasqs = sbt("biasqs", [128, 32], F32)
    biasbh = sbt("biasbh", [128, 32], F32)
    biaso = sbt("biaso", [128, 32], F32)
    small = sbt("small", [128, 64], F32)
    sumab = sbt("sumab", [128, 20], F32)

    psA = st.enter_context(nc.psum_tensor("psA", [128, 2048], F32))
    psB = st.enter_context(nc.psum_tensor("psB", [128, 2048], F32))

    pan = [bigB[:, 0:4096], bigB[:, 4096:8192]]
    ksq = bigB[:, 8192:9216]
    qsq = bigB[:, 8192:9216]
    xsq = [bigB[:, 9216 + 0 * T: 9216 + 1 * T],
           bigB[:, 9216 + 1 * T: 9216 + 2 * T],
           bigB[:, 8192:9216]]
    esb = bigB[0:40, 11264:12288]
    stag = [bigB[:, 12288 + i * T: 12288 + (i + 1) * T] for i in range(2)]
    vstg = [bigB[:, 14336 + i * T: 14336 + (i + 1) * T] for i in range(2)]
    msb = [bigB[:, 16384 + i * 4096: 16384 + (i + 1) * 4096].bitcast(F16) for i in range(2)]
    fsb = [bigB[:, 24576 + i * 4096: 24576 + (i + 1) * 4096] for i in range(2)]
    masku = bigB[:, 0:1920].bitcast(U8)
    xres = [fb[:, i * T:(i + 1) * T] for i in range(2)]
    ysb = [fb[:, 2048 + i * T: 2048 + (i + 1) * T] for i in range(2)]
    Ab = fb[:, 4096:5120]
    Bb = fb[:, 5120:6144]
    tsc = fb[:, 6144:7168]
    coeff = fb[:, 0:3840]
    gsb = fb[:, 0:640].bitcast(BF16)
    maskf = fb[:, 3840:7680]
    meanr = fb[0:1, 0:1024]
    varr = fb[0:1, 1024:2048]
    rstdr = fb[0:1, 2048:3072]
    brow = fb[0:1, 3072:4096]
    kmax2c = small[:, 0:1]
    negc = small[:, 8:16]
    csq = small[:, 16:24]
    ccol = small[:, 24:32]
    svecs = small[:, 32:64]
    kmx = small[0:1, 48:49]
    kred = small[0:1, 49:50]
    kmg = small[0:1, 50:51]
    km8 = small[0:1, 52:60]
    epsc = small[0:1, 61:62]
    kbuf = kvb[:, 0:4096]
    vbuf = kvb[:, 4096:8192]

    S = lambda name: Sem(st, nc, name)
    gld, sld, tld = S("gld"), S("sld"), S("tld")
    stx, sty = S("stx"), S("sty")
    pe, act, dve = S("pe"), S("act"), S("dve")
    cc1, cc2 = S("cc1"), S("cc2")

    prog = {k: [] for k in ("sync", "gpsimd", "scalar", "vector", "tensor")}

    def on(eng, fn):
        prog[eng].append(fn)

    def wait(eng, *pairs):
        pairs = tuple((s.h, v) for s, v in pairs if v > 0)
        if pairs:
            on(eng, lambda e, p=pairs: [e.wait_ge(sh, v) for sh, v in p])

    def p0(e):
        e.dma_start(out=encs[:, :], in_=encr[:, :]).then_inc(gld.h, 16)
        e.dma_start(out=coeff, in_=coefr[:, :]).then_inc(gld.h, 16)
        e.dma_start(out=masku, in_=maskr[:, :]).then_inc(gld.h, 16)
        e.dma_start(out=svecs, in_=svec_in[:, :]).then_inc(gld.h, 16)
        e.dma_start(out=biasq[:, :], in_=qkvb_c[:, :]).then_inc(gld.h, 16)
        e.dma_start(out=biasbh[:, :], in_=bb_c[:, :]).then_inc(gld.h, 16)
        e.dma_start(out=biaso[:, :], in_=ob_c[:, :]).then_inc(gld.h, 16)
    on("gpsimd", p0)
    gld.take(16 * 7)
    P0_DONE = gld.n

    def p1(e):
        cf = coeff.rearrange("p (oc ee r) -> p oc ee r", oc=32, ee=3)
        mk = masku.rearrange("p (oc ee r) -> p oc ee r", oc=32, ee=3)
        mf = maskf.rearrange("p (oc ee r) -> p oc ee r", oc=32, ee=3)
        e.tensor_copy(mf[:, :, :, :], mk[:, :, :, :])
        e.tensor_tensor(mf[:, :, :, :], cf[:, :, :, :], mf[:, :, :, :], ALU.mult)
        e.tensor_tensor(mf[:, :, 0, :], mf[:, :, 0, :], mf[:, :, 1, :], ALU.add)
        e.tensor_tensor(mf[:, :, 0, :], mf[:, :, 0, :], mf[:, :, 2, :], ALU.add)
        for oc in range(32):
            e.tensor_scalar(gsb[:, oc * 40:(oc + 1) * 40], mf[:, oc, 0, :],
                            svecs[:, oc:oc + 1], None, ALU.mult)
        e.memset(onec[:, :], 1.0)
        e.memset(oner[:, :], 1.0)
        e.memset(epsc, EPS)
        e.tensor_scalar(biasqs[:, :], biasq[:, 0:32], QSCALE, None, ALU.mult)
        e.tensor_scalar(biasbh[:, :], biasbh[:, :], 0.5, None,
                        ALU.mult).then_inc(dve.h, 1)
    wait("vector", (gld, P0_DONE))
    on("vector", p1)
    GPREP = dve.take(1)

    sin_done, gmm_done = [], []
    for oc in range(32):
        par = oc % 2
        if oc >= 2:
            wait("sync", (act, sin_done[oc - 2]))
        on("sync", lambda e, oc=oc, par=par: e.dma_start(
            out=msb[par][:, :], in_=m16[oc, :, :]).then_inc(sld.h, 16))
        ml = sld.take(16)
        wait("scalar", (sld, ml))
        if oc >= 2:
            wait("scalar", (pe, gmm_done[oc - 2]))
        on("scalar", lambda e, par=par: e.activation(
            fsb[par][:, :], msb[par][:, :], AF.Sin, scale=PI_OVER).then_inc(act.h, 1))
        sin_done.append(act.take(1))
        wait("tensor", (act, sin_done[oc]), (dve, GPREP))
        def gmm(e, oc=oc, par=par):
            for ts in range(8):
                ps = (psA if ts < 4 else psB)[0:40, (ts % 4) * 512:(ts % 4 + 1) * 512]
                last = e.matmul(ps, gsb[:, oc * 40:(oc + 1) * 40],
                                fsb[par][:, ts * 512:(ts + 1) * 512],
                                start=(oc == 0), stop=(oc == 31), skip_group_check=True)
            last.then_inc(pe.h, 1)
        on("tensor", gmm)
        gmm_done.append(pe.take(1))
    wait("vector", (pe, gmm_done[-1]))
    def gt_evac(e):
        for ts in range(8):
            ps = (psA if ts < 4 else psB)[0:40, (ts % 4) * 512:(ts % 4 + 1) * 512]
            e.tensor_copy(gt[0:40, ts * 512:(ts + 1) * 512], ps)
        e.tensor_copy(small[0:40, 56:60], gt[0:40, 4092:4096]).then_inc(dve.h, 1)
    on("vector", gt_evac)
    GT_READY = dve.take(1)

    def ln_and_hat(src_dram, store_gate=None):
        loads = []
        for dc in range(NCH):
            if store_gate is not None and dc == 0:
                wait("gpsimd", (store_gate[0], store_gate[1]))
            on("gpsimd", lambda e, dc=dc: e.dma_start(
                out=bigA[:, dc * T:(dc + 1) * T],
                in_=src_dram[dc * 128:(dc + 1) * 128, :]).then_inc(gld.h, 16))
            loads.append(gld.take(16))
        on("gpsimd", lambda e: e.dma_start(
            out=bigA[:, (NCH - 1) * T:NCH * T],
            in_=src_dram[(NCH - 1) * 128:NCH * 128, :]).then_inc(gld.h, 16))
        loads.append(gld.take(16))
        stats_mm = []
        sq_done = []
        for it in range(NCH + 1):
            if it < NCH:
                dc_s = it
                wait("scalar", (gld, loads[dc_s + 1]))
                if dc_s >= 3:
                    wait("scalar", (pe, stats_mm[dc_s - 3]))
                def sq_em(e, par=dc_s % 3, dc=dc_s):
                    e.activation(xsq[par][:, :], bigA[:, dc * T:(dc + 1) * T], AF.Square)
                    e.activation(small[:, 56:60], xsq[par][:, 1020:1024],
                                 AF.Identity).then_inc(act.h, 1)
                on("scalar", sq_em)
                sq_done.append(act.take(1))
            else:
                wait("scalar", (pe, stats_mm[NCH - 3]))
                on("scalar", lambda e: e.activation(
                    xsq[NCH % 3][:, :], bigA[:, 0:T], AF.Square).then_inc(act.h, 1))
                sq_done.append(act.take(1))
            if it == 0:
                continue
            dc = it - 1
            par = dc % 3
            sqd = sq_done[it]
            if dc == 0:
                wait("tensor", (act, sqd), (dve, GT_READY))
            else:
                wait("tensor", (act, sqd))
            def smm(e, dc=dc, par=par):
                for h2 in range(2):
                    e.matmul(psB[0:1, h2 * 512:(h2 + 1) * 512], onec[:, :],
                             bigA[:, dc * T + h2 * 512: dc * T + (h2 + 1) * 512],
                             start=(dc == 0), stop=(dc == NCH - 1), skip_group_check=True)
                    e.matmul(psB[0:1, 1024 + h2 * 512: 1024 + (h2 + 1) * 512],
                             onec[:, :], xsq[par][:, h2 * 512:(h2 + 1) * 512],
                             start=(dc == 0), stop=(dc == NCH - 1), skip_group_check=True)
                if dc == NCH - 1:
                    e.matmul(psA[0:1, 0:512], onec[:, :], bigA[:, 0:512],
                             start=True, stop=True, skip_group_check=True)
                    e.matmul(psA[0:1, 0:512], onec[:, :], bigA[:, 0:512],
                             start=True, stop=True, skip_group_check=True)
                    last = e.matmul(psA[0:1, 0:512], onec[:, :], bigA[:, 0:512],
                                    start=True, stop=True, skip_group_check=True)
                else:
                    last = e.matmul(psA[0:1, 0:1], onec[:, :], onec[:, :], start=True,
                                    stop=True, skip_group_check=True)
                last.then_inc(pe.h, 1)
            on("tensor", smm)
            stats_mm.append(pe.take(1))
        wait("vector", (pe, stats_mm[-1]))
        vs = [dve.take(1) for _ in range(4)]
        def sm1(e, vs=vs):
            e.tensor_scalar(meanr, psB[0:1, 0:1024], INV_D, None, ALU.mult).then_inc(dve.h, 1)
            e.wait_ge(dve.h, vs[0])
            e.tensor_scalar(varr, psB[0:1, 1024:2048], INV_D, None, ALU.mult).then_inc(dve.h, 1)
            e.wait_ge(dve.h, vs[1])
            e.tensor_tensor(brow, meanr, meanr, ALU.mult).then_inc(dve.h, 1)
            e.wait_ge(dve.h, vs[2])
            e.tensor_tensor(varr, varr, brow, ALU.subtract).then_inc(dve.h, 1)
            e.wait_ge(dve.h, vs[3])
            e.tensor_copy(small[0:1, 56:60], varr[0:1, 1020:1024]).then_inc(dve.h, 1)
        on("vector", sm1)
        v1 = dve.take(1)
        wait("scalar", (dve, v1))
        def sa1(e):
            e.activation(rstdr, varr, AF.Ln, bias=epsc)
            e.activation(rstdr, rstdr, AF.Exp, scale=-0.5)
            e.activation(small[0:1, 56:60], rstdr[0:1, 1020:1024],
                         AF.Identity).then_inc(act.h, 1)
        on("scalar", sa1)
        a1 = act.take(1)
        wait("vector", (act, a1))
        vb = dve.take(1)
        def bm(e, vb=vb):
            e.tensor_tensor(brow, meanr, rstdr, ALU.mult).then_inc(dve.h, 1)
            e.wait_ge(dve.h, vb)
            e.tensor_copy(small[0:1, 56:60], brow[0:1, 1020:1024]).then_inc(dve.h, 1)
        on("vector", bm)
        b1 = dve.take(1)
        wait("tensor", (dve, b1))
        def bc(e):
            for h2 in range(2):
                e.matmul(psB[:, h2 * 512:(h2 + 1) * 512], oner[:, :],
                         rstdr[:, h2 * 512:(h2 + 1) * 512], start=True, stop=True,
                         skip_group_check=True)
                e.matmul(psB[:, 1024 + h2 * 512: 1024 + (h2 + 1) * 512], oner[:, :],
                         brow[:, h2 * 512:(h2 + 1) * 512], start=True, stop=True,
                         skip_group_check=True)
            e.matmul(psA[0:1, 0:1], onec[:, :], onec[:, :], start=True, stop=True,
                     skip_group_check=True)
            last = e.matmul(psA[0:1, 0:1], onec[:, :], onec[:, :], start=True,
                            stop=True, skip_group_check=True)
            last.then_inc(pe.h, 1)
        on("tensor", bc)
        bc_done = pe.take(1)
        wait("vector", (pe, bc_done))
        def bce(e):
            e.tensor_copy(Ab[:, :], psB[:, 0:1024])
            e.tensor_copy(Bb[:, :], psB[:, 1024:2048])
            e.tensor_copy(small[:, 56:58], Ab[:, 1022:1024])
            e.tensor_copy(small[:, 58:60], Bb[:, 1022:1024]).then_inc(dve.h, 1)
        on("vector", bce)
        bce_done = dve.take(1)
        hat = []
        for dc in range(NCH):
            if dc == 0:
                wait("vector", (dve, bce_done))
            def xh(e, dc=dc):
                e.tensor_tensor(tsc[:, :], bigA[:, dc * T:(dc + 1) * T], Ab[:, :], ALU.mult)
                e.tensor_tensor(bigA[:, dc * T:(dc + 1) * T], tsc[:, :], Bb[:, :],
                                ALU.subtract)
                e.tensor_copy(small[:, 56:60], bigA[:, dc * T + 1020: dc * T + 1024]
                              ).then_inc(dve.h, 1)
            on("vector", xh)
            hat.append(dve.take(1))
        return {"hat": hat, "bc_done": bc_done, "stats_mm": stats_mm}

    def ffn(src_dram, dst_dram, dst_sem, store_gate=None, xres_extra=(), stop_at_hat=False, dump_dbg=False):
        ms = ln_and_hat(src_dram, store_gate=store_gate)
        hat_all = ms["hat"][-1]
        if stop_at_hat:
            wait("gpsimd", (dve, hat_all))
            b_sty = sty.n
            def mvh3(e, b_sty=b_sty):
                for dc in range(NCH):
                    e.dma_start(out=yT[dc * 128:(dc + 1) * 128, :],
                                in_=bigA[:, dc * T:(dc + 1) * T]).then_inc(sty.h, 16)
                e.wait_ge(sty.h, b_sty + NCH * 16)
            on("gpsimd", mvh3)
            sty.take(16 * NCH)
            return None
        if dump_dbg:
            wait("sync", (dve, ms["hat"][0]))
            b_sty3 = sty.n
            def mvrows(e, b_sty3=b_sty3):
                e.dma_start(out=dbg3[0:1, :], in_=meanr).then_inc(sty.h, 16)
                e.dma_start(out=dbg3[1:2, :], in_=varr).then_inc(sty.h, 16)
                e.dma_start(out=dbg3[2:3, :], in_=rstdr).then_inc(sty.h, 16)
                e.dma_start(out=dbg3[3:4, :], in_=brow).then_inc(sty.h, 16)
                e.dma_start(out=dbg3[4:132, :], in_=Ab[:, :]).then_inc(sty.h, 16)
                e.wait_ge(sty.h, b_sty3 + 80)
            on("sync", mvrows)
            sty.take(80)
            wait("gpsimd", (dve, hat_all))
            b_gld = gld.n
            def mvdbg(e, b_gld=b_gld):
                for dc in range(NCH):
                    e.dma_start(out=dbg[dc * 128:(dc + 1) * 128, :],
                                in_=bigA[:, dc * T:(dc + 1) * T]).then_inc(gld.h, 16)
            on("gpsimd", mvdbg)
            DBG_SNAP = gld.take(16 * NCH)
            wait("tensor", (gld, DBG_SNAP))
        wait("tensor", (dve, hat_all))
        def emm(e):
            for dc in range(NCH):
                for h2 in range(2):
                    e.matmul(psB[0:40, h2 * 512:(h2 + 1) * 512],
                             encs[:, dc * 40:(dc + 1) * 40],
                             bigA[:, dc * T + h2 * 512: dc * T + (h2 + 1) * 512],
                             start=(dc == 0), stop=(dc == NCH - 1),
                             skip_group_check=True)
            e.matmul(psA[0:1, 0:1], onec[:, :], onec[:, :], start=True, stop=True,
                     skip_group_check=True)
            last = e.matmul(psA[0:1, 0:1], onec[:, :], onec[:, :], start=True,
                            stop=True, skip_group_check=True)
            last.then_inc(pe.h, 1)
        on("tensor", emm)
        e_done = pe.take(1)
        wait("vector", (pe, e_done))
        def esbc(e):
            e.tensor_copy(esb[:, :], psB[0:40, 0:1024])
            e.tensor_copy(small[0:40, 56:60], esb[0:40, 1020:1024]).then_inc(dve.h, 1)
        on("vector", esbc)
        esb_done = dve.take(1)

        mm_done = [0] * NCH
        evac_done = [0] * NCH
        st_done = [0] * NCH
        for oc in range(NCH):
            par = oc % 2
            if oc >= 2:
                wait("gpsimd", (pe, mm_done[oc - 2]))
            def panld(e, oc=oc, par=par):
                e.dma_start(out=pan[par][:, :], in_=wb[oc, :, :]).then_inc(gld.h, 16)
                e.dma_start(out=pan[par][:, 3968:4096],
                            in_=wb[oc, :, 3968:4096]).then_inc(gld.h, 16)
            on("gpsimd", panld)
            wl = gld.take(32)
            if oc >= 2:
                wait("sync", (dve, evac_done[oc - 2]))
            elif xres_extra:
                wait("sync", *xres_extra)
            if oc < 2:
                wait("sync", (pe, ms["bc_done"]))
            def xrl(e, oc=oc, par=par):
                e.dma_start(out=xres[par][:, :],
                            in_=src_dram[oc * 128:(oc + 1) * 128, :]).then_inc(sld.h, 16)
                e.dma_start(out=xres[par][:, 896:1024],
                            in_=src_dram[oc * 128:(oc + 1) * 128, 896:1024]
                            ).then_inc(sld.h, 16)
            on("sync", xrl)
            xl = sld.take(32)
            wait("tensor", (gld, wl), (dve, esb_done))
            if oc >= 2:
                wait("tensor", (dve, evac_done[oc - 2]))
            def fmm(e, oc=oc, par=par):
                po = psA[:, par * 1024:(par + 1) * 1024]
                for dc in range(NCH):
                    for h2 in range(2):
                        e.matmul(po[:, h2 * 512:(h2 + 1) * 512],
                                 pan[par][:, dc * 128:(dc + 1) * 128],
                                 bigA[:, dc * T + h2 * 512: dc * T + (h2 + 1) * 512],
                                 start=(dc == 0), stop=False, skip_group_check=True)
                for h2 in range(2):
                    last = e.matmul(po[:, h2 * 512:(h2 + 1) * 512],
                             gt[0:40, oc * 128:(oc + 1) * 128],
                             esb[0:40, h2 * 512:(h2 + 1) * 512],
                             start=False, stop=True, skip_group_check=True)
                last.then_inc(pe.h, 1)
            on("tensor", fmm)
            mm_done[oc] = pe.take(1)
            if oc >= 2:
                wait("vector", (pe, mm_done[oc]), (sld, xl), (dst_sem, st_done[oc - 2]))
            else:
                wait("vector", (pe, mm_done[oc]), (sld, xl))
            def fev(e, oc=oc, par=par):
                po = psA[:, par * 1024:(par + 1) * 1024]
                e.tensor_scalar(ysb[par][:, :], po, 0.5, biasbh[:, oc:oc + 1],
                                ALU.mult, ALU.add)
                e.tensor_tensor(ysb[par][:, :], ysb[par][:, :], xres[par][:, :],
                                ALU.add)
                e.tensor_copy(small[:, 56:60], ysb[par][:, 1020:1024]).then_inc(dve.h, 1)
            on("vector", fev)
            evac_done[oc] = dve.take(1)
            wait("sync", (dve, evac_done[oc]))
            on("sync", lambda e, oc=oc, par=par: e.dma_start(
                out=dst_dram[oc * 128:(oc + 1) * 128, :], in_=ysb[par][:, :]
            ).then_inc(dst_sem.h, 16))
            st_done[oc] = dst_sem.take(16)
        return {"mm_done": mm_done, "evac_done": evac_done, "st_done": st_done}

    def dump_and_stop(src_dram, gate_sem, gate_val):
        wait("sync", (gate_sem, gate_val))
        b_sld, b_sty = sld.n, sty.n
        def mv(e, b_sld=b_sld, b_sty=b_sty):
            for dc in range(NCH):
                e.dma_start(out=xres[0][:, :],
                            in_=src_dram[dc * 128:(dc + 1) * 128, :]).then_inc(sld.h, 16)
                e.wait_ge(sld.h, b_sld + (dc + 1) * 16)
                e.dma_start(out=yT[dc * 128:(dc + 1) * 128, :],
                            in_=xres[0][:, :]).then_inc(sty.h, 16)
                e.wait_ge(sty.h, b_sty + (dc + 1) * 16)
        on("sync", mv)
        sld.take(16 * NCH)
        sty.take(16 * NCH)

    def finish():
        with nc.Block() as block:
            block.sync(lambda e: [f(e) for f in prog["sync"]])
            block.gpsimd(lambda e: [f(e) for f in prog["gpsimd"]])
            block.scalar(lambda e: [f(e) for f in prog["scalar"]])
            block.vector(lambda e: [f(e) for f in prog["vector"]])
            block.tensor(lambda e: [f(e) for f in prog["tensor"]])
        st.close()
        return nc

    if stop_phase == "gprep":
        wait("gpsimd", (dve, GT_READY))
        on("gpsimd", lambda e: e.dma_start(
            out=yT[0:160, :].rearrange("(r c) t -> r c t", c=4),
            in_=gt[0:40, :].rearrange("r (c t) -> r c t", c=4)).then_inc(sty.h, 16))
        wait("sync", (sty, sty.take(16)))
        on("sync", lambda e: e.nop())
        return finish()

    if stop_phase == "loads1":
        loads = []
        for dc in range(NCH):
            on("gpsimd", lambda e, dc=dc: e.dma_start(
                out=bigA[:, dc * T:(dc + 1) * T],
                in_=xT[dc * 128:(dc + 1) * 128, :]).then_inc(gld.h, 16))
            loads.append(gld.take(16))
        wait("gpsimd", (gld, loads[-1]))
        b_sty = sty.n
        def mvh0(e, b_sty=b_sty):
            for dc in range(NCH):
                e.dma_start(out=yT[dc * 128:(dc + 1) * 128, :],
                            in_=bigA[:, dc * T:(dc + 1) * T]).then_inc(sty.h, 16)
            e.wait_ge(sty.h, b_sty + NCH * 16)
        on("gpsimd", mvh0)
        sty.take(16 * NCH)
        return finish()

    if stop_phase in ("sq1", "mm1", "mmacc1"):
        loads = []
        for dc in range(NCH):
            on("gpsimd", lambda e, dc=dc: e.dma_start(
                out=bigA[:, dc * T:(dc + 1) * T],
                in_=xT[dc * 128:(dc + 1) * 128, :]).then_inc(gld.h, 16))
            loads.append(gld.take(16))
        if stop_phase == "sq1":
            sq_d = []
            for dc in range(NCH):
                par = dc % 2
                wait("scalar", (gld, loads[dc]))
                on("scalar", lambda e, par=par, dc=dc: e.activation(
                    xsq[par][:, :], bigA[:, dc * T:(dc + 1) * T], AF.Square
                ).then_inc(act.h, 1))
                sq_d.append(act.take(1))
            wait("gpsimd", (act, sq_d[-1]))
            b_sty = sty.n
            def mv1(e, b_sty=b_sty):
                e.dma_start(out=yT[0:128, :], in_=xsq[1][:, :]).then_inc(sty.h, 16)
                e.wait_ge(sty.h, b_sty + 16)
            on("gpsimd", mv1)
            sty.take(16)
            return finish()
        acc = stop_phase == "mmacc1"
        mms = []
        for dc in range(NCH):
            wait("tensor", (gld, loads[dc]), (dve, GT_READY))
            def mmz(e, dc=dc, acc=acc):
                for h2 in range(2):
                    last = e.matmul(psB[0:1, h2 * 512:(h2 + 1) * 512], onec[:, :],
                             bigA[:, dc * T + h2 * 512: dc * T + (h2 + 1) * 512],
                             start=(dc == 0 if acc else True),
                             stop=(dc == NCH - 1 if acc else True),
                             skip_group_check=True)
                last.then_inc(pe.h, 1)
            on("tensor", mmz)
            mms.append(pe.take(1))
        wait("vector", (pe, mms[-1]))
        on("vector", lambda e: e.tensor_copy(fb[0:1, 0:1024].bitcast(F32), psB[0:1, 0:1024]).then_inc(dve.h, 1))
        sd = dve.take(1)
        wait("sync", (dve, sd))
        b_sty = sty.n
        def mv2(e, b_sty=b_sty):
            e.dma_start(out=yT[0:1, 0:1024], in_=fb[0:1, 0:1024]).then_inc(sty.h, 16)
            e.wait_ge(sty.h, b_sty + 16)
        on("sync", mv2)
        sty.take(16)
        return finish()

    if stop_phase == "stats1":
        loads = []
        for dc in range(NCH):
            on("gpsimd", lambda e, dc=dc: e.dma_start(
                out=bigA[:, dc * T:(dc + 1) * T],
                in_=xT[dc * 128:(dc + 1) * 128, :]).then_inc(gld.h, 16))
            loads.append(gld.take(16))
        stats_mm = []
        for dc in range(NCH):
            par = dc % 2
            wait("scalar", (gld, loads[dc]))
            if dc >= 2:
                wait("scalar", (pe, stats_mm[dc - 2]))
            def sq_em(e, par=par, dc=dc):
                e.activation(xsq[par][:, :], bigA[:, dc * T:(dc + 1) * T], AF.Square)
                e.activation(small[:, 56:60], xsq[par][:, 1020:1024],
                             AF.Identity).then_inc(act.h, 1)
            on("scalar", sq_em)
            sqd = act.take(1)
            wait("tensor", (act, sqd))
            def smmz(e, dc=dc, par=par):
                for h2 in range(2):
                    e.matmul(psB[0:1, h2 * 512:(h2 + 1) * 512], onec[:, :],
                             bigA[:, dc * T + h2 * 512: dc * T + (h2 + 1) * 512],
                             start=(dc == 0), stop=(dc == NCH - 1), skip_group_check=True)
                    last = e.matmul(psB[0:1, 1024 + h2 * 512: 1024 + (h2 + 1) * 512],
                             onec[:, :], xsq[par][:, h2 * 512:(h2 + 1) * 512],
                             start=(dc == 0), stop=(dc == NCH - 1), skip_group_check=True)
                last.then_inc(pe.h, 1)
            on("tensor", smmz)
            stats_mm.append(pe.take(1))
        wait("vector", (pe, stats_mm[-1]), (dve, GPREP))
        on("vector", lambda e: e.tensor_copy(fb[0:1, 0:2048].bitcast(F32), psB[0:1, 0:2048]).then_inc(dve.h, 1))
        sd = dve.take(1)
        wait("sync", (dve, sd))
        b_sty = sty.n
        def mvs(e, b_sty=b_sty):
            e.dma_start(out=yT[0:1, 0:1024], in_=fb[0:1, 0:1024]).then_inc(sty.h, 16)
            e.dma_start(out=yT[1:2, 0:1024], in_=fb[0:1, 1024:2048]).then_inc(sty.h, 16)
            e.wait_ge(sty.h, b_sty + 32)
        on("sync", mvs)
        sty.take(32)
        return finish()

    if stop_phase == "hat1":
        msx = ln_and_hat(xT)
        wait("gpsimd", (dve, msx["hat"][-1]))
        b_sty = sty.n
        def mvh(e, b_sty=b_sty):
            for dc in range(NCH):
                e.dma_start(out=yT[dc * 128:(dc + 1) * 128, :],
                            in_=bigA[:, dc * T:(dc + 1) * T]).then_inc(sty.h, 16)
            e.wait_ge(sty.h, b_sty + NCH * 16)
        on("gpsimd", mvh)
        sty.take(16 * NCH)
        return finish()

    if stop_phase == "wpan1":
        wait("gpsimd", (dve, GT_READY))
        on("gpsimd", lambda e: e.dma_start(out=pan[0][:, :], in_=wb[0, :, :]).then_inc(gld.h, 16))
        wl = gld.take(16)
        b_sty = sty.n
        def mvw(e, b_sty=b_sty, wl=wl):
            e.wait_ge(gld.h, wl)
            for dc in range(4):
                e.dma_start(out=yT[dc * 128:(dc + 1) * 128, 0:128],
                            in_=pan[0][:, dc * 128:(dc + 1) * 128]).then_inc(sty.h, 16)
            e.wait_ge(sty.h, b_sty + 4 * 16)
        on("gpsimd", mvw)
        sty.take(16 * 4)
        return finish()

    f1 = ffn(xT, x1, stx, xres_extra=((dve, GPREP),))
    FFN1_STORES = stx.n
    if stop_phase == "ffn1":
        dump_and_stop(x1[:, :], stx, FFN1_STORES)
        return finish()

    wait("gpsimd", (pe, f1["mm_done"][-1]))
    ms2 = ln_and_hat(x1, store_gate=(stx, FFN1_STORES))
    hat2_all = ms2["hat"][-1]

    qmm_done = [0] * 96
    qev_done = [0] * 96
    qst_done = [0] * 96
    vtr_done = {}
    kn_red = {}
    kn_mm = {}
    for oc in range(96):
        par = oc % 2
        if oc >= 2:
            wait("gpsimd", (pe, qmm_done[oc - 2]))
        def panld2(e, oc=oc, par=par):
            e.dma_start(out=pan[par][:, :], in_=wqkv[oc, :, :]).then_inc(gld.h, 16)
            e.dma_start(out=pan[par][:, 3968:4096],
                        in_=wqkv[oc, :, 3968:4096]).then_inc(gld.h, 16)
        on("gpsimd", panld2)
        wl = gld.take(32)
        wait("tensor", (gld, wl), (dve, hat2_all))
        if oc >= 2:
            wait("tensor", (act, qev_done[oc - 2]))
        def qmm(e, oc=oc, par=par):
            po = psA[:, par * 1024:(par + 1) * 1024]
            for dc in range(NCH):
                for h2 in range(2):
                    last = e.matmul(po[:, h2 * 512:(h2 + 1) * 512],
                             pan[par][:, dc * 128:(dc + 1) * 128],
                             bigA[:, dc * T + h2 * 512: dc * T + (h2 + 1) * 512],
                             start=(dc == 0), stop=(dc == NCH - 1), skip_group_check=True)
            last.then_inc(pe.h, 1)
        on("tensor", qmm)
        qmm_done[oc] = pe.take(1)
        w = [(pe, qmm_done[oc])]
        if oc >= 2:
            w.append((stx, qst_done[oc - 2]))
            if oc - 2 >= 64:
                w.append((tld, vtr_done[oc - 2]))
            if 32 <= oc - 2 < 64:
                w.append((pe, kn_mm[oc - 2]))
        wait("scalar", *w)
        def qev(e, oc=oc, par=par):
            if oc < 32:
                e.activation(stag[par][:, :], psA[:, par * 1024:(par + 1) * 1024],
                             AF.Identity, bias=biasqs[:, oc:oc + 1], scale=QSCALE)
            else:
                e.activation(stag[par][:, :], psA[:, par * 1024:(par + 1) * 1024],
                             AF.Identity, bias=biasq[:, oc:oc + 1])
            e.activation(small[:, 56:60], stag[par][:, 1020:1024],
                         AF.Identity).then_inc(act.h, 1)
        on("scalar", qev)
        qev_done[oc] = act.take(1)
        if oc < 32:
            wait("sync", (act, qev_done[oc]))
            on("sync", lambda e, oc=oc, par=par: e.dma_start(
                out=qspill[oc, :, :], in_=stag[par][:, :]).then_inc(stx.h, 16))
            qst_done[oc] = stx.take(16)
        elif oc < 64:
            kc = oc - 32
            wait("sync", (act, qev_done[oc]))
            on("sync", lambda e, kc=kc, par=par: e.dma_start(
                out=kv_in.ap()[kc * 128:(kc + 1) * 128, :], in_=stag[par][:, :]
            ).then_inc(stx.h, 16))
            qst_done[oc] = stx.take(16)
            prev_kn = kn_mm.get(oc - 1)
            if prev_kn:
                wait("scalar", (pe, prev_kn))
            def ksq_em(e, par=par):
                e.activation(ksq[:, :], stag[par][:, :], AF.Square)
                e.activation(small[:, 56:60], ksq[:, 1020:1024],
                             AF.Identity).then_inc(act.h, 1)
            on("scalar", ksq_em)
            ksd = act.take(1)
            w2 = [(act, ksd)]
            if kc >= 1:
                w2.append((dve, kn_red[oc - 1]))
            wait("tensor", *w2)
            def knmm(e):
                for h2 in range(2):
                    e.matmul(psB[0:1, 1024 + h2 * 512: 1024 + (h2 + 1) * 512],
                             onec[:, :], ksq[:, h2 * 512:(h2 + 1) * 512],
                             start=True, stop=True, skip_group_check=True)
                e.matmul(psB[0:1, 512:513], onec[:, :], onec[:, :], start=True,
                         stop=True, skip_group_check=True)
                last = e.matmul(psB[0:1, 512:513], onec[:, :], onec[:, :], start=True,
                                stop=True, skip_group_check=True)
                last.then_inc(pe.h, 1)
            on("tensor", knmm)
            kn_mm[oc] = pe.take(1)
            wait("vector", (pe, kn_mm[oc]))
            vk = dve.take(1)
            def knr(e, kc=kc, vk=vk):
                e.reduce_max(kred, psB[0:1, 1024:2048], AXX).then_inc(dve.h, 1)
                e.wait_ge(dve.h, vk)
                if kc == 0:
                    e.tensor_copy(kmx, kred).then_inc(dve.h, 1)
                else:
                    e.tensor_tensor(kmx, kmx, kred, ALU.max).then_inc(dve.h, 1)
            on("vector", knr)
            kn_red[oc] = dve.take(1)
        else:
            vc = oc - 64
            wait("scalar", (act, qev_done[oc]))
            if oc >= 2 and oc - 2 >= 64:
                wait("scalar", (stx, qst_done[oc - 2]))
            def vtr(e, par=par):
                src = stag[par][:, :].rearrange("d (tc t) -> d tc t", tc=8)
                dst = vstg[par][:, :].rearrange("t (tc d) -> t tc d", tc=8)
                e.dma_start(out=dst, in_=src, transpose=True).then_inc(tld.h, 16)
            on("scalar", vtr)
            vtr_done[oc] = tld.take(16)
            wait("sync", (tld, vtr_done[oc]))
            on("sync", lambda e, vc=vc, par=par: e.dma_start(
                out=kv_in.ap()[4096 + vc * 128: 4096 + (vc + 1) * 128, :],
                in_=vstg[par][:, :]).then_inc(stx.h, 16))
            qst_done[oc] = stx.take(16)
    QKV_STORES = stx.n
    KN_LAST = kn_red[63]

    wait("sync", (dve, KN_LAST))
    on("sync", lambda e: e.dma_start(out=km_in.ap()[:, :], in_=kmx).then_inc(stx.h, 16))
    KM_STORE = stx.take(16)
    wait("gpsimd", (stx, QKV_STORES))
    on("gpsimd", lambda e: e.collective_compute(
        "AllGather", ALU.bypass, replica_groups=[[0, 1, 2, 3], [4, 5, 6, 7]],
        ins=[kv_in.ap().opt()], outs=[kv_out.ap().opt()]).then_inc(cc1.h))
    cc1.take(1)
    wait("gpsimd", (stx, KM_STORE))
    on("gpsimd", lambda e: e.collective_compute(
        "AllGather", ALU.bypass, replica_groups=[[0, 1, 2, 3, 4, 5, 6, 7]],
        ins=[km_in.ap().opt()], outs=[km_out.ap().opt()]).then_inc(cc2.h))
    cc2.take(1)

    wait("sync", (cc2, 1))
    on("sync", lambda e: e.dma_start(
        out=km8, in_=km_out.ap().rearrange("r one -> one r")).then_inc(sld.h, 16))
    kml = sld.take(16)
    wait("vector", (sld, kml))
    on("vector", lambda e: e.reduce_max(kmg, km8, AXX).then_inc(dve.h, 1))
    kmr = dve.take(1)
    wait("tensor", (dve, kmr), (act, qev_done[95]))
    def kmbc(e):
        e.matmul(psB[:, 0:1], oner[:, :], kmg, start=True, stop=True,
                 skip_group_check=True)
        e.matmul(psB[0:1, 512:513], onec[:, :], onec[:, :], start=True, stop=True,
                 skip_group_check=True)
        e.matmul(psB[0:1, 512:513], onec[:, :], onec[:, :], start=True, stop=True,
                 skip_group_check=True).then_inc(pe.h, 1)
    on("tensor", kmbc)
    kmb = pe.take(1)
    wait("vector", (pe, kmb))
    on("vector", lambda e: e.tensor_copy(kmax2c, psB[:, 0:1]).then_inc(dve.h, 1))
    KMAXC = dve.take(1)
    if stop_phase == "qkv":
        wait("gpsimd", (cc1, 1), (cc2, 1))
        b_gld, b_sty = gld.n, sty.n
        def mvq(e, b_gld=b_gld, b_sty=b_sty):
            for dc in range(NCH):
                e.dma_start(out=xres[0][:, :],
                            in_=kv_in.ap()[dc * 128:(dc + 1) * 128, :]).then_inc(gld.h, 16)
                e.wait_ge(gld.h, b_gld + (dc + 1) * 16)
                e.dma_start(out=yT[dc * 128:(dc + 1) * 128, :],
                            in_=xres[0][:, :]).then_inc(sty.h, 16)
                e.wait_ge(sty.h, b_sty + (dc + 1) * 16)
        on("gpsimd", mvq)
        gld.take(16 * NCH)
        sty.take(16 * NCH)
        return finish()

    if stop_phase == "attq":
        wait("gpsimd", (dve, KMAXC))
        b_sty = sty.n
        def mvq2(e, b_sty=b_sty):
            for hh in range(4):
                e.dma_start(out=yT[hh * 128:(hh + 1) * 128, :],
                            in_=qspill[hh, :, :]).then_inc(sty.h, 16)
            e.dma_start(out=yT[512:513, 0:1], in_=kmg).then_inc(sty.h, 16)
            e.dma_start(out=yT[513:514, 0:8], in_=km8).then_inc(sty.h, 16)
            e.dma_start(out=yT[514:515, 0:1], in_=kmx).then_inc(sty.h, 16)
            e.wait_ge(sty.h, b_sty + 16 * 7)
        on("gpsimd", mvq2)
        sty.take(16 * 7)
        return finish()

    k_ld = [0] * NH
    v_ld = [0] * NH
    q_ld = [0] * NH
    sc_done = [[0] * 8 for _ in range(NH)]
    ex_done = [[0] * 8 for _ in range(NH)]
    nm_done = [[0] * 8 for _ in range(NH)]
    tr_done = [[0] * 8 for _ in range(NH)]
    ctx_done = [0] * NH
    ctxev = [0] * NH

    def kvq_load(h):
        if h == 0:
            wait("sync", (cc1, 1))
        else:
            wait("sync", (pe, sc_done[h - 1][7]))
        def kl(e, h=h):
            for r in range(4):
                e.dma_start(out=kbuf[:, r * 1024:(r + 1) * 1024],
                            in_=kv_out.ap()[r * 8192 + h * 128: r * 8192 + (h + 1) * 128, :]
                            ).then_inc(sld.h, 16)
            e.dma_start(out=kbuf[:, 3 * 1024 + 896: 4 * 1024],
                        in_=kv_out.ap()[3 * 8192 + h * 128: 3 * 8192 + (h + 1) * 128, 896:1024]
                        ).then_inc(sld.h, 16)
        on("sync", kl)
        k_ld[h] = sld.take(80)
        if h > 0:
            wait("sync", (pe, ctx_done[h - 1]))
        def vl(e, h=h):
            for r in range(4):
                e.dma_start(out=vbuf[:, r * 1024:(r + 1) * 1024],
                            in_=kv_out.ap()[r * 8192 + 4096 + h * 128:
                                            r * 8192 + 4096 + (h + 1) * 128, :]
                            ).then_inc(sld.h, 16)
            e.dma_start(out=vbuf[:, 3 * 1024 + 896: 4 * 1024],
                        in_=kv_out.ap()[3 * 8192 + 4096 + h * 128:
                                        3 * 8192 + 4096 + (h + 1) * 128, 896:1024]
                        ).then_inc(sld.h, 16)
        on("sync", vl)
        v_ld[h] = sld.take(80)
        qpar = h % 2
        def ql(e, h=h, qpar=qpar):
            e.dma_start(out=qh[:, qpar * T:(qpar + 1) * T],
                        in_=qspill[h, :, :]).then_inc(sld.h, 16)
            e.dma_start(out=qh[:, qpar * T + 896: qpar * T + 1024],
                        in_=qspill[h, :, 896:1024]).then_inc(sld.h, 16)
        on("sync", ql)
        q_ld[h] = sld.take(32)

    NH_RUN = 1 if stop_phase == "attn1" else NH
    kvq_load(0)
    for h in range(NH_RUN):
        qpar = h % 2
        qhv = qh[:, qpar * T:(qpar + 1) * T]
        w = [(sld, q_ld[h])]
        if h >= 1:
            w.append((pe, ctx_done[h - 1]))
        wait("scalar", *w)
        def qsq_em(e, qhv=qhv):
            e.activation(qsq[:, :], qhv, AF.Square)
            e.activation(small[:, 56:60], qsq[:, 1020:1024],
                         AF.Identity).then_inc(act.h, 1)
        on("scalar", qsq_em)
        qsqd = act.take(1)
        w = [(act, qsqd)]
        if h >= 1:
            w.append((act, ex_done[h - 1][7]))
            w.append((dve, ctxev[h - 1]))
        wait("tensor", *w)
        def qnmm(e):
            for qt in range(8):
                e.matmul(psA[:, qt:qt + 1], qsq[:, qt * 128:(qt + 1) * 128],
                         onec[:, :], start=True, stop=True, skip_group_check=True)
            e.matmul(psA[0:1, 8:9], onec[:, :], onec[:, :], start=True, stop=True,
                     skip_group_check=True)
            last = e.matmul(psA[0:1, 8:9], onec[:, :], onec[:, :], start=True,
                            stop=True, skip_group_check=True)
            last.then_inc(pe.h, 1)
        on("tensor", qnmm)
        qn = pe.take(1)
        wait("vector", (pe, qn), (dve, KMAXC))
        on("vector", lambda e: e.tensor_scalar(csq, psA[:, 0:8], kmax2c, None,
                                               ALU.mult).then_inc(dve.h, 1))
        cs = dve.take(1)
        wait("scalar", (dve, cs))
        on("scalar", lambda e: e.activation(ccol, csq, AF.Sqrt,
                                            scale=MARGIN2).then_inc(act.h, 1))
        cd = act.take(1)
        wait("vector", (act, cd))
        on("vector", lambda e: e.tensor_scalar(negc, ccol, -1.0, None,
                                               ALU.mult).then_inc(dve.h, 1))
        ncd = dve.take(1)

        nm_emit = []
        for qt in range(8):
            ppar = qt % 2
            pv = pbuf[:, ppar * 4096:(ppar + 1) * 4096]
            w = [(sld, k_ld[h]), (dve, ncd)]
            if qt >= 1:
                w.append((act, ex_done[h][qt - 1]))
            elif h >= 1:
                w.append((act, ex_done[h - 1][7]))
                w.append((dve, ctxev[h - 1]))
            wait("tensor", *w)
            def smm2(e, qhv=qhv, qt=qt):
                for ks in range(8):
                    ps = (psA if ks < 4 else psB)[:, (ks % 4) * 512:(ks % 4 + 1) * 512]
                    last = e.matmul(ps, qhv[:, qt * 128:(qt + 1) * 128],
                             kbuf[:, ks * 512:(ks + 1) * 512],
                             start=True, stop=True, skip_group_check=True)
                last.then_inc(pe.h, 1)
            on("tensor", smm2)
            sc_done[h][qt] = pe.take(1)
            w = [(pe, sc_done[h][qt])]
            if qt >= 1:
                w.append((dve, nm_done[h][qt - 1]))
            elif h >= 1:
                w.append((dve, nm_done[h - 1][7]))
            if qt >= 2:
                w.append((tld, tr_done[h][qt - 2]))
            elif h >= 1:
                w.append((tld, tr_done[h - 1][qt + 6]))
            wait("scalar", *w)
            def expp(e, pv=pv, qt=qt):
                e.activation(pv[:, 0:2048], psA[:, :], AF.Exp,
                             bias=negc[:, qt:qt + 1], accum_out=sumab[:, 0:1])
                e.activation(pv[:, 2048:4096], psB[:, :], AF.Exp,
                             bias=negc[:, qt:qt + 1],
                             accum_out=sumab[:, 1:2]).then_inc(act.h, 1)
            on("scalar", expp)
            ex_done[h][qt] = act.take(1)
            wait("vector", (act, ex_done[h][qt]))
            v1 = dve.take(1)
            v2 = dve.take(1)
            def pnorm(e, pv=pv, v1=v1, v2=v2):
                e.tensor_tensor(sumab[:, 2:3], sumab[:, 0:1], sumab[:, 1:2],
                                ALU.add).then_inc(dve.h, 1)
                e.wait_ge(dve.h, v1)
                e.reciprocal(sumab[:, 3:4], sumab[:, 2:3]).then_inc(dve.h, 1)
                e.wait_ge(dve.h, v2)
                e.tensor_scalar(pv[:, :], pv[:, :], sumab[:, 3:4], None,
                                ALU.mult).then_inc(dve.h, 1)
            on("vector", pnorm)
            nm_done[h][qt] = dve.take(1)
            w = [(dve, nm_done[h][qt])]
            if h >= 1 and qt == 0:
                w.append((pe, ctx_done[h - 1]))
            wait("scalar", *w)
            def ptr(e, pv=pv, qt=qt):
                src = pv.rearrange("q (kc k) -> q kc k", kc=32)
                dst = bigB[:, :].rearrange("k (kc qq) -> k kc qq", kc=32)[:, :, qt * 128:(qt + 1) * 128]
                e.dma_start(out=dst, in_=src, transpose=True).then_inc(tld.h, 16)
            on("scalar", ptr)
            tr_done[h][qt] = tld.take(16)

        wait("tensor", (tld, tr_done[h][7]), (sld, v_ld[h]), (act, ex_done[h][7]))
        def cmm(e):
            for kc in range(32):
                for h2 in range(2):
                    e.matmul(psA[:, h2 * 512:(h2 + 1) * 512],
                             vbuf[:, kc * 128:(kc + 1) * 128],
                             bigB[:, kc * 1024 + h2 * 512: kc * 1024 + (h2 + 1) * 512],
                             start=(kc == 0), stop=(kc == 31), skip_group_check=True)
            e.matmul(psB[0:1, 0:1], onec[:, :], onec[:, :], start=True, stop=True,
                     skip_group_check=True)
            last = e.matmul(psB[0:1, 0:1], onec[:, :], onec[:, :], start=True,
                            stop=True, skip_group_check=True)
            last.then_inc(pe.h, 1)
        on("tensor", cmm)
        ctx_done[h] = pe.take(1)
        wait("vector", (pe, ctx_done[h]))
        def cev(e, h=h):
            e.tensor_copy(bigA[:, h * T: h * T + 512], psA[:, 0:512])
            e.tensor_copy(bigA[:, h * T + 512:(h + 1) * T], psA[:, 512:1024])
            e.tensor_copy(small[:, 56:60], bigA[:, h * T + 1020: h * T + 1024]
                          ).then_inc(dve.h, 1)
        on("vector", cev)
        ctxev[h] = dve.take(1)
        if h + 1 < NH_RUN:
            kvq_load(h + 1)

    if stop_phase == "attn1":
        wait("gpsimd", (dve, ctxev[0]))
        b_sty = sty.n
        def mvp(e, b_sty=b_sty):
            for cc in range(NCH):
                e.dma_start(out=yT[cc * 128:(cc + 1) * 128, :],
                            in_=bigB[:, cc * T:(cc + 1) * T]).then_inc(sty.h, 16)
            e.wait_ge(sty.h, b_sty + 16 * NCH)
        on("gpsimd", mvp)
        sty.take(16 * NCH)
        return finish()

    if stop_phase == "attn":
        wait("gpsimd", (dve, ctxev[NH - 1]))
        b_sty = sty.n
        def mva(e, b_sty=b_sty):
            for hh in range(NH):
                e.dma_start(out=yT[hh * 128:(hh + 1) * 128, :],
                            in_=bigA[:, hh * T:(hh + 1) * T]).then_inc(sty.h, 16)
            e.wait_ge(sty.h, b_sty + 16 * NH)
        on("gpsimd", mva)
        sty.take(16 * NH)
        return finish()

    omm_done = [0] * NCH
    oev_done = [0] * NCH
    ost_done = [0] * NCH
    for oc in range(NCH):
        par = oc % 2
        if oc >= 2:
            wait("gpsimd", (pe, omm_done[oc - 2]))
        else:
            wait("gpsimd", (pe, ctx_done[NH - 1]))
        def panld3(e, oc=oc, par=par):
            e.dma_start(out=pan[par][:, :], in_=wo[oc, :, :]).then_inc(gld.h, 16)
            e.dma_start(out=pan[par][:, 3968:4096],
                        in_=wo[oc, :, 3968:4096]).then_inc(gld.h, 16)
        on("gpsimd", panld3)
        wl = gld.take(32)
        if oc >= 2:
            wait("sync", (dve, oev_done[oc - 2]))
        else:
            wait("sync", (dve, f1["evac_done"][-1]))
        def xrl2(e, oc=oc, par=par):
            e.dma_start(out=xres[par][:, :],
                        in_=x1[oc * 128:(oc + 1) * 128, :]).then_inc(sld.h, 16)
            e.dma_start(out=xres[par][:, 896:1024],
                        in_=x1[oc * 128:(oc + 1) * 128, 896:1024]).then_inc(sld.h, 16)
        on("sync", xrl2)
        xl = sld.take(32)
        wait("tensor", (gld, wl), (dve, ctxev[NH - 1]))
        if oc >= 2:
            wait("tensor", (dve, oev_done[oc - 2]))
        def omm(e, oc=oc, par=par):
            po = psB[:, par * 1024:(par + 1) * 1024]
            for dc in range(NCH):
                for h2 in range(2):
                    last = e.matmul(po[:, h2 * 512:(h2 + 1) * 512],
                             pan[par][:, dc * 128:(dc + 1) * 128],
                             bigA[:, dc * T + h2 * 512: dc * T + (h2 + 1) * 512],
                             start=(dc == 0), stop=(dc == NCH - 1), skip_group_check=True)
            last.then_inc(pe.h, 1)
        on("tensor", omm)
        omm_done[oc] = pe.take(1)
        if oc >= 2:
            wait("vector", (pe, omm_done[oc]), (sld, xl), (stx, ost_done[oc - 2]))
        else:
            wait("vector", (pe, omm_done[oc]), (sld, xl))
        def oev(e, oc=oc, par=par):
            po = psB[:, par * 1024:(par + 1) * 1024]
            e.scalar_tensor_tensor(ysb[par][:, :], po, biaso[:, oc:oc + 1],
                                   xres[par][:, :], ALU.add, ALU.add)
            e.tensor_copy(small[:, 56:60], ysb[par][:, 1020:1024]).then_inc(dve.h, 1)
        on("vector", oev)
        oev_done[oc] = dve.take(1)
        wait("sync", (dve, oev_done[oc]))
        on("sync", lambda e, oc=oc, par=par: e.dma_start(
            out=x2[oc * 128:(oc + 1) * 128, :], in_=ysb[par][:, :]).then_inc(stx.h, 16))
        ost_done[oc] = stx.take(16)
    OUTPROJ_STORES = stx.n
    if stop_phase == "proj":
        dump_and_stop(x2[:, :], stx, OUTPROJ_STORES)
        return finish()

    wait("gpsimd", (pe, omm_done[-1]))
    GT_READY = oev_done[-1]
    r7 = ffn(x2, yT, sty, store_gate=(stx, OUTPROJ_STORES),
             xres_extra=((dve, oev_done[-1]),),
             stop_at_hat=(stop_phase == "hat3"),
             dump_dbg=(stop_phase == "fulldbg"))
    if stop_phase == "hat3":
        return finish()
    Y_STORES = sty.n
    if stop_phase == "fulldbg":
        wait("sync", (sty, Y_STORES))
        b_sld2, b_sty2 = sld.n, sty.n
        def mvx2(e, b_sld2=b_sld2, b_sty2=b_sty2):
            for dc in range(NCH):
                e.dma_start(out=xres[0][:, :],
                            in_=x2[dc * 128:(dc + 1) * 128, :]).then_inc(sld.h, 16)
                e.wait_ge(sld.h, b_sld2 + (dc + 1) * 16)
                e.dma_start(out=dbg2[dc * 128:(dc + 1) * 128, :],
                            in_=xres[0][:, :]).then_inc(sty.h, 16)
                e.wait_ge(sty.h, b_sty2 + (dc + 1) * 16)
        on("sync", mvx2)
        sld.take(16 * NCH)
        sty.take(16 * NCH)

    wait("sync", (sty, Y_STORES))
    on("sync", lambda e: e.nop())
    return finish()


_M16, _SVEC = None, None
_NC_CACHE = None


def _host_prep(inputs):
    global _M16, _SVEC
    if _M16 is None:
        _M16, _SVEC = _m16_table()
    x = np.ascontiguousarray(inputs["L0_attn_x"], np.float32).reshape(8192, 4096)
    qkvw = np.ascontiguousarray(inputs["L0_attn_attn_qkvw"], np.float32)
    qkvb = np.ascontiguousarray(inputs["L0_attn_attn_qkvb"], np.float32)
    ow = np.ascontiguousarray(inputs["L0_attn_attn_ow"], np.float32)
    ob = np.ascontiguousarray(inputs["L0_attn_attn_ob"], np.float32)
    Wb = np.ascontiguousarray(inputs["L0_ffn_W_base"], np.float32)
    bb = np.ascontiguousarray(inputs["L0_ffn_b_base"], np.float32)
    enc0 = np.ascontiguousarray(inputs["L0_ffn_enc0"], np.float32)
    enc1 = np.ascontiguousarray(inputs["L0_ffn_enc1"], np.float32)
    c0 = np.ascontiguousarray(inputs["L0_ffn_coefs0"], np.float32)
    c1 = np.ascontiguousarray(inputs["L0_ffn_coefs1"], np.float32)
    m0 = inputs["L0_ffn_mask0"]
    m1 = inputs["L0_ffn_mask1"]

    wqkv_r = np.ascontiguousarray(
        qkvw.reshape(32, 128, 96, 128).transpose(2, 1, 0, 3)).reshape(96, 128, 4096)
    wb_r = np.ascontiguousarray(
        Wb.reshape(32, 128, 32, 128).transpose(0, 3, 2, 1)).reshape(32, 128, 4096)
    wo_r = np.ascontiguousarray(
        ow.reshape(32, 128, 32, 128).transpose(2, 1, 0, 3)).reshape(32, 128, 4096)
    enc = np.concatenate([enc0, enc1], 0)
    enc_r = np.ascontiguousarray(
        enc.reshape(40, 32, 128).transpose(2, 1, 0)).reshape(128, 32 * 40)
    cfull = np.concatenate(
        [c0.reshape(3, 32, 128, 32), c1.reshape(3, 32, 128, 8)], -1)
    coef_r = np.ascontiguousarray(cfull.transpose(2, 1, 0, 3)).reshape(128, 32 * 3 * 40)
    mfull = np.concatenate(
        [m0.reshape(3, 32, 128, 32), m1.reshape(3, 32, 128, 8)], -1)
    mask_r = np.ascontiguousarray(
        mfull.transpose(2, 1, 0, 3)).reshape(128, 32 * 3 * 40).view(np.uint8)
    qkvb_c = np.ascontiguousarray(qkvb.reshape(96, 128).T)
    bb_c = np.ascontiguousarray(bb.reshape(32, 128).T)
    ob_c = np.ascontiguousarray(ob.reshape(32, 128).T)

    shared = {"wqkv": wqkv_r, "wb": wb_r, "wo": wo_r, "m16": _M16,
              "encr": enc_r, "coefr": coef_r, "maskr": mask_r, "svec": _SVEC,
              "qkvb_c": qkvb_c, "bb_c": bb_c, "ob_c": ob_c}
    in_maps = []
    for c in range(8):
        d = dict(shared)
        d["xT"] = np.ascontiguousarray(x[c * 1024:(c + 1) * 1024, :].T)
        in_maps.append(d)
    return in_maps


def kernel(**inputs):
    global _NC_CACHE
    in_maps = _host_prep(inputs)
    if _NC_CACHE is None:
        _NC_CACHE = build_nc()
    res = run_bass_kernel_spmd(_NC_CACHE, in_maps, core_ids=list(range(8)))
    parts = [np.ascontiguousarray(np.asarray(r["yT"]).T) for r in res.results]
    return np.concatenate(parts, 0).reshape(2, 4096, 4096).astype(np.float32)
